# version 08
# baseline (speedup 1.0000x reference)
import os
import numpy as np
import ml_dtypes
from contextlib import ExitStack
import concourse.bass as bass
import concourse.mybir as mybir
from concourse.bass_utils import run_bass_kernel_spmd

F32 = mybir.dt.float32
BF16 = mybir.dt.bfloat16
I32 = mybir.dt.int32
AF = mybir.ActivationFunctionType
ALU = mybir.AluOpType

ENGINES = ("sync", "tensor", "vector", "scalar", "gpsimd")

H = 2048
HC = 16
NH = 16
DK = 128
TOWN = 1024
HALO = 128
TT = TOWN + HALO
NTB = TT // 128
C = 64
NCH = TT // C
INTER = 5504
NOB = 43
GOB_PER = 6
DOB_PER = 2
LNC_Q = float(-0.5 * np.log(128.0))
SQRT_DK = float(np.sqrt(128.0))
EPS = 1e-5


class Prog:

    def __init__(self, nc, st):
        self.nc = nc
        self.st = st
        self.chain = {e: st.enter_context(nc.semaphore(f"ch_{e}")) for e in ENGINES}
        self.dsems = {}
        self.blocks = [[]]
        self.res = {}
        self.ops = []
        self.dtot = {}
        self.fenced = {}

    def _dsem(self, key):
        if key not in self.dsems:
            self.dsems[key] = self.st.enter_context(self.nc.semaphore(f"dq_{key}"))
        return self.dsems[key]

    def _key(self, item):
        return ("r", item)

    def op(self, eng, fn, reads=(), writes=(), dma=False, dsem=None, asyn=False):
        idx = len(self.ops)
        deps = set()
        rkeys = [self._key(r) for r in reads]
        wkeys = [self._key(w) for w in writes]
        for k in rkeys:
            v = self.res.get(k)
            if v and v["w"] is not None:
                deps.add(v["w"])
        for k in wkeys:
            v = self.res.get(k)
            if v:
                if v["w"] is not None:
                    deps.add(v["w"])
                deps.update(v["r"])
        o = dict(eng=eng, fn=fn, dma=dma, deps=deps, idx=idx, waited=False,
                 cnt=None, blk=len(self.blocks) - 1, asyn=asyn or dma,
                 dsem=(dsem or f"_dq_{eng}") if dma else None)
        if dma:
            self.dtot[o["dsem"]] = self.dtot.get(o["dsem"], 0) + 16
        self.ops.append(o)
        self.blocks[-1].append(o)
        for k in rkeys:
            self.res.setdefault(k, {"w": None, "r": []})["r"].append(idx)
        for k in wkeys:
            e = self.res.setdefault(k, {"w": None, "r": []})
            e["w"] = idx
            e["r"] = []
        return idx

    def new_block(self):
        pend = {k: v for k, v in self.dtot.items() if v > self.fenced.get(k, 0)}
        if pend:
            def fence(s, pend=dict(pend)):
                inst = None
                for k, v in pend.items():
                    inst = s.wait_ge(self._dsem(k), v)
                return inst
            self.op("sync", fence)
            self.fenced.update(pend)
        if self.blocks[-1]:
            self.blocks.append([])

    def emit(self):
        for o in self.ops:
            if o["dma"]:
                o["waited"] = True
            real = set()
            for d in o["deps"]:
                p = self.ops[d]
                if p["blk"] < o["blk"] and not p["asyn"]:
                    continue
                if (p["eng"] == "tensor" and o["eng"] == "tensor"
                        and not p["dma"] and not o["dma"]):
                    continue
                real.add(d)
                p["waited"] = True
            o["rdeps"] = real
        cnt = {e: 0 for e in ENGINES}
        dcnt = {}
        for o in self.ops:
            if o["dma"]:
                dcnt[o["dsem"]] = dcnt.get(o["dsem"], 0) + 16
                o["cnt"] = dcnt[o["dsem"]]
            elif o["waited"]:
                cnt[o["eng"]] += 1
                o["cnt"] = cnt[o["eng"]]
        run = {}
        for o in self.ops:
            o["_run"] = dict(run)
            if o["dma"]:
                run[o["dsem"]] = run.get(o["dsem"], 0) + 16
        for blk in self.blocks:
            hi = {}
            for o in blk:
                o["waits"] = []
                for d in sorted(o["rdeps"]):
                    p = self.ops[d]
                    if p["dma"]:
                        semkey = ("d", p["dsem"])
                        val = o["_run"].get(p["dsem"], p["cnt"])
                        sem = self._dsem(p["dsem"])
                    else:
                        semkey = ("c", p["eng"])
                        val = p["cnt"]
                        sem = self.chain[p["eng"]]
                    key = (o["eng"], semkey)
                    if hi.get(key, 0) >= val:
                        continue
                    hi[key] = val
                    o["waits"].append((sem, val))
            byeng = {}
            for o in blk:
                byeng.setdefault(o["eng"], []).append(o)
            with self.nc.Block() as block:
                for eng, ops in byeng.items():
                    def body(engine, ops=ops):
                        for o in ops:
                            for sem, val in o["waits"]:
                                engine.wait_ge(sem, val)
                            inst = o["fn"](engine)
                            if o["dma"]:
                                inst.then_inc(self._dsem(o["dsem"]), 16)
                            elif o["waited"]:
                                inst.then_inc(self.chain[o["eng"]], 1)
                    getattr(block, eng)(body)


def build():
    nc = bass.Bass("TRN2", target_bir_lowering=False)

    x_own = nc.dram_tensor("x_own", [TOWN, H], F32, kind="ExternalInput")
    x_halo = nc.dram_tensor("x_halo", [HALO, H], F32, kind="ExternalInput")
    wq_d = nc.dram_tensor("wq", [H, H], F32, kind="ExternalInput")
    wf_d = nc.dram_tensor("wf", [H, H], F32, kind="ExternalInput")
    wi_d = nc.dram_tensor("wi", [H, H], F32, kind="ExternalInput")
    wo_d = nc.dram_tensor("wo", [H, H], F32, kind="ExternalInput")
    gwc_d = nc.dram_tensor("gwc", [128, 16], F32, kind="ExternalInput")
    lut_d = nc.dram_tensor("lut16", [1, 16], F32, kind="ExternalInput")
    gidx_d = nc.dram_tensor("gidxs", [H, GOB_PER * 128], I32, kind="ExternalInput")
    uidx_d = nc.dram_tensor("uidxs", [H, GOB_PER * 128], I32, kind="ExternalInput")
    didx_d = nc.dram_tensor("didxs", [INTER, DOB_PER * 128], I32, kind="ExternalInput")
    y_d = nc.dram_tensor("y", [TOWN, H], F32, kind="ExternalOutput")

    ccg_i = nc.dram_tensor("ccg_i", [H, GOB_PER * 128], BF16)
    ccu_i = nc.dram_tensor("ccu_i", [H, GOB_PER * 128], BF16)
    ccd_i = nc.dram_tensor("ccd_i", [INTER, DOB_PER * 128], BF16)
    ccg_o = nc.dram_tensor("ccg_o", [8 * H, GOB_PER * 128], BF16)
    ccu_o = nc.dram_tensor("ccu_o", [8 * H, GOB_PER * 128], BF16)
    ccd_o = nc.dram_tensor("ccd_o", [8 * INTER, DOB_PER * 128], BF16)
    xfm_d = nc.dram_tensor("xfm_d", [128, HC, TT], F32)
    x1p_d = nc.dram_tensor("x1p_d", [128, HC, TOWN], F32)
    x1n_d = nc.dram_tensor("x1n_d", [128, HC, TOWN], F32)
    x2p_d = nc.dram_tensor("x2p_d", [128, HC, TOWN], F32)

    ident_bf = nc.inline_tensor(np.eye(128, dtype=ml_dtypes.bfloat16), name="idbf")
    ident_f = nc.inline_tensor(np.eye(128, dtype=np.float32), name="idf")
    triu_c = nc.inline_tensor(np.triu(np.ones((64, 64), np.float32)), name="triu")
    onesb_c = nc.inline_tensor(np.ones((128, 1), ml_dtypes.bfloat16), name="onesb")
    ones1_c = nc.inline_tensor(np.ones((1, 128), np.float32), name="ones1")

    with ExitStack() as st:
        e = st.enter_context
        P = Prog(nc, st)

        AR_COLS = 44032
        arena = e(nc.sbuf_tensor("arena", [128, AR_COLS], F32))
        arf = arena.ap()
        arb = arena.ap().bitcast(BF16)

        def cvf(off_b, n):
            assert off_b % 4 == 0
            return arf[:, off_b // 4: off_b // 4 + n]

        def cvb(off_b, n):
            assert off_b % 2 == 0
            return arb[:, off_b // 2: off_b // 2 + n]

        idbf_s = e(nc.sbuf_tensor("idbf_s", [128, 128], BF16))
        idf_s = e(nc.sbuf_tensor("idf_s", [128, 128], F32))
        triu_s = e(nc.sbuf_tensor("triu_s", [64, 64], F32))
        onesb_s = e(nc.sbuf_tensor("onesb_s", [128, 1], BF16))
        ones1_s = e(nc.sbuf_tensor("ones1_s", [1, 128], F32))
        gwc_s = e(nc.sbuf_tensor("gwc_s", [128, 16], F32))
        lrow_s = e(nc.sbuf_tensor("lrow_s", [1, 16], F32))
        lrep_s = e(nc.sbuf_tensor("lrep_s", [128, 16], F32))
        smask_s = e(nc.sbuf_tensor("smask_s", [128, 2304], F32))
        Hst = e(nc.sbuf_tensor("Hst", [128, 256], F32))
        Hb = e(nc.sbuf_tensor("Hb", [128, 256], BF16))
        khs = e(nc.sbuf_tensor("khs", [128, 128], BF16))
        khts = e(nc.sbuf_tensor("khts", [64, 256], BF16))
        sbfs = e(nc.sbuf_tensor("sbfs", [64, 256], BF16))
        lame = e(nc.sbuf_tensor("lame", [128, 36], F32))
        rows = e(nc.sbuf_tensor("rows", [1, 2048], F32))
        rstdr = e(nc.sbuf_tensor("rstdr", [128, 1024], F32))
        cb = e(nc.sbuf_tensor("cb", [128, 2], F32))
        mat_i = e(nc.sbuf_tensor("mat_i", [128, 1536], BF16))
        mat_a = e(nc.sbuf_tensor("mat_a", [128, 1536], BF16))
        mat_t = e(nc.sbuf_tensor("mat_t", [128, 1536], BF16))

        PB = [e(nc.psum_tensor(f"pb{i}", [128, 512], F32)) for i in range(7)]
        PBF = e(nc.psum_tensor("pbf", [128, 1024], BF16))

        o_bf = cvb(94208, NH * TOWN).rearrange("p (h t) -> p h t", t=TOWN)
        x1_bf = cvb(126976, HC * TOWN).rearrange("p (h t) -> p h t", t=TOWN)
        h_bf = cvb(0, NOB * TOWN).rearrange("p (h t) -> p h t", t=TOWN)
        x_bf = cvb(0, HC * TT).rearrange("p (h t) -> p h t", t=TT)
        xstg = [cvf(94208, 2048), cvf(102400, 2048)]
        xtmp = cvf(110592, 2048)

        wset = {}
        for i, w in enumerate(("f", "q", "i")):
            for par in range(2):
                off = 45056 + (i * 2 + par) * 8192
                wset[(w, par)] = cvb(off, HC * 256).rearrange("p (h c) -> p h c", c=256)
        g_f = cvf(126976, 2 * TT).rearrange("p (h t) -> p h t", t=TT)
        g2d = cvf(126976, 2 * TT)
        G_f = cvf(136192, 2 * TT).rearrange("p (h t) -> p h t", t=TT)
        G2d = cvf(136192, 2 * TT)
        q_b = cvb(145408, 2 * TOWN).rearrange("p (h t) -> p h t", t=TOWN)
        vpool = cvb(149504, NCH * 256).rearrange("p (c v) -> p c v", v=256)
        lq_b = cvb(159744, 2 * TT).rearrange("p (h t) -> p h t", t=TT)
        lk_b = cvb(164352, 2 * TT).rearrange("p (h t) -> p h t", t=TT)
        k_b = cvb(168960, 2 * TT).rearrange("p (h t) -> p h t", t=TT)
        lq2d = cvb(159744, 2 * TT)
        lk2d = cvb(164352, 2 * TT)
        k2d = cvb(168960, 2 * TT)
        tA = cvf(173568, 512)

        P.op("sync", lambda s: s.dma_start(idbf_s[:, :], ident_bf.ap()), writes=["idbf"], dma=True, dsem="cst")
        P.op("sync", lambda s: s.dma_start(idf_s[:, :], ident_f.ap()), writes=["idf"], dma=True, dsem="cst")
        P.op("sync", lambda s: s.dma_start(triu_s[:, :], triu_c.ap()), writes=["triu"], dma=True, dsem="cst")
        P.op("sync", lambda s: s.dma_start(onesb_s[:, :], onesb_c.ap()), writes=["onesb"], dma=True, dsem="cst")
        P.op("sync", lambda s: s.dma_start(ones1_s[:, :], ones1_c.ap()), writes=["ones1"], dma=True, dsem="cst")
        P.op("sync", lambda s: s.dma_start(gwc_s[:, :], gwc_d.ap()), writes=["gwc"], dma=True, dsem="cst")
        P.op("sync", lambda s: s.dma_start(lrow_s[:, :], lut_d.ap()), writes=["lrow"], dma=True, dsem="cst")
        P.op("tensor", lambda t: t.matmul(PB[6][:, 0:16], ones1_s[:, :], lrow_s[:, :], start=True, stop=True),
             reads=["ones1", "lrow"], writes=["P60"])
        P.op("vector", lambda v: v.tensor_copy(lrep_s[:, :], PB[6][:, 0:16]),
             reads=["P60"], writes=["lrep"])
        P.op("vector", lambda v: v.memset(smask_s[:, :], 1.0), writes=["smask"])
        sm3 = smask_s.ap().rearrange("p (c t) -> p c t", t=64)
        P.op("vector", lambda v: v.memset(sm3[:, :, 0:1], 0.0), reads=["smask"], writes=["smask"])
        P.op("vector", lambda v: v.memset(cb[:, 0:1], LNC_Q), writes=["cb"])
        P.op("vector", lambda v: v.memset(cb[:, 1:2], EPS), reads=["cb"], writes=["cb"])

        for tb in range(NTB):
            par = tb % 2
            src = x_halo.ap() if tb == 0 else x_own.ap()[(tb - 1) * 128: tb * 128, :]
            P.op("sync", lambda s, src=src, par=par: s.dma_start(xstg[par][:, :], src),
                 writes=[f"xstg{par}"], dma=True, dsem=f"xs{par}")
            for hc in range(HC):
                tp = hc % 2 if not os.environ.get("BK_XSEQ") else 0
                P.op("tensor", lambda t, par=par, hc=hc, tp=tp: t.transpose(
                        PB[6][:, tp * 128:(tp + 1) * 128], xstg[par][:, hc * 128:(hc + 1) * 128], idf_s[:, :]),
                     reads=[f"xstg{par}", "idf"], writes=[f"P6{tp}"])
                P.op("vector", lambda v, hc=hc, tp=tp: v.tensor_copy(
                        xtmp[:, hc * 128:(hc + 1) * 128], PB[6][:, tp * 128:(tp + 1) * 128]),
                     reads=[f"P6{tp}"], writes=["xtmp"])
                P.op("scalar", lambda s, hc=hc, tb=tb: s.copy(x_bf[:, hc, tb * 128:(tb + 1) * 128],
                                                              xtmp[:, hc * 128:(hc + 1) * 128]),
                     reads=["xtmp"], writes=["x_bf"])
            P.op("sync", lambda s, tb=tb: s.dma_start(
                    xfm_d.ap()[:, :, tb * 128:(tb + 1) * 128],
                    xtmp.rearrange("p (h t) -> p h t", t=128)),
                 reads=["xtmp"], writes=["xfmd"], dma=True, dsem="xfd")

        def mk_mat_ops(src_d, dst_d, r0, nr, ncols, tagp):
            n = nr * ncols
            src = src_d.ap()[r0 * 128:(r0 + nr) * 128, :].rearrange("(c p) o -> p c o", p=128)
            yield lambda: P.op("gpsimd", lambda g: g.dma_start(
                    mat_i[:, 0:n].rearrange("p (c o) -> p c o", o=ncols), src),
                 writes=["mati"], dma=True, dsem="mi")
            yield lambda: P.op("vector", lambda v: v.tensor_scalar(
                    mat_a[:, 0:n], mat_i[:, 0:n], 0.0, lrep_s[:, 0:1], ALU.is_equal, ALU.mult),
                 reads=["mati", "lrep"], writes=["mata"])
            for j in range(1, 16):
                def two(j=j):
                    P.op("vector", lambda v: v.tensor_scalar(mat_t[:, 0:n], mat_i[:, 0:n], float(j),
                                                             lrep_s[:, j:j + 1], ALU.is_equal, ALU.mult),
                         reads=["mati", "lrep"], writes=["matt"])
                    P.op("vector", lambda v: v.tensor_add(mat_a[:, 0:n], mat_a[:, 0:n], mat_t[:, 0:n]),
                         reads=["mata", "matt"], writes=["mata"])
                yield two
            dst = dst_d.ap()[r0 * 128:(r0 + nr) * 128, :].rearrange("(c p) o -> p c o", p=128)
            yield lambda: P.op("sync", lambda s: s.dma_start(
                    dst, mat_a[:, 0:n].rearrange("p (c o) -> p c o", o=ncols)),
                 reads=["mata"], writes=[tagp + "_w"], dma=True, dsem=tagp)

        def mk_cc_ops(cin, cout, tagp, otag):
            yield lambda: P.op("gpsimd", lambda g: g.collective_compute(
                    "AllGather", ALU.bypass, replica_groups=[list(range(8))],
                    ins=[cin.ap().opt()], outs=[cout.ap().opt()]),
                 reads=[tagp + "_w"], writes=[otag], asyn=True)

        def mat_all():
            for r0 in range(0, HC, 2):
                yield from mk_mat_ops(gidx_d, ccg_i, r0, 2, 768, "ccg")
            yield from mk_cc_ops(ccg_i, ccg_o, "ccg", "ccg_o")
            for r0 in range(0, HC, 2):
                yield from mk_mat_ops(uidx_d, ccu_i, r0, 2, 768, "ccu")
            yield from mk_cc_ops(ccu_i, ccu_o, "ccu", "ccu_o")
            for r0 in range(0, 42, 6):
                yield from mk_mat_ops(didx_d, ccd_i, r0, 6, 256, "ccd")
            yield from mk_mat_ops(didx_d, ccd_i, 42, 1, 256, "ccd")
            yield from mk_cc_ops(ccd_i, ccd_o, "ccd", "ccd_o")

        mat_it = iter(mat_all())

        def mat_pump(k=1):
            for _ in range(k):
                f_ = next(mat_it, None)
                if f_ is None:
                    return
                f_()

        def fslice(tg):
            return (0, 512) if tg == 0 else ((512, 1024) if tg == 1 else (1024, 1152))

        def wload(si):
            par = si % 2
            for w, wd in (("f", wf_d), ("q", wq_d), ("i", wi_d)):
                src = wd.ap()[:, si * 256:(si + 1) * 256].rearrange("(h p) c -> p h c", p=128)
                P.op("gpsimd", lambda g, w=w, par=par, src=src: g.dma_start(wset[(w, par)][:, :, :], src),
                     writes=[f"w{w}{par}"], dma=True, dsem=f"w{w}{par}")

        def proj_units(si):
            par = si % 2
            units = []
            OLDPS = bool(os.environ.get("BK_OLDPS"))
            OLDF = OLDPS or bool(os.environ.get("BK_OLDF"))
            OLDQ = OLDPS or bool(os.environ.get("BK_OLDQ"))
            tiles = [(cg, tgi) for cg in range(2) for tgi in range(3)]
            for ti, (cg, tgi) in enumerate(tiles):
                def fgrp(cg=cg, tgi=tgi, ti=ti):
                    lo, hi = fslice(tgi)
                    pi = 4 + ti % 2
                    pp = PB[pi]
                    wtags = [f"P{pi}"]
                    for hc in range(HC):
                        P.op("tensor", lambda t, cg=cg, hc=hc, lo=lo, hi=hi, pp=pp, par=par:
                             t.matmul(pp[:, 0:hi - lo], wset[("f", par)][:, hc, cg * 128:(cg + 1) * 128],
                                      x_bf[:, hc, lo:hi], start=(hc == 0), stop=(hc == HC - 1)),
                             reads=[f"wf{par}", "x_bf"], writes=wtags)
                    P.op("scalar", lambda sc, cg=cg, lo=lo, hi=hi, pp=pp, pi=pi:
                         sc.copy(g_f[:, cg, lo:hi], pp[:, 0:hi - lo]),
                         reads=[f"P{pi}"], writes=["g_f"])
                units.append(fgrp)
            QUNITS = []
            for cg in range(2):
                for tgi in range(2):
                    def qgrp(cg=cg, tgi=tgi):
                        lo, hi = 128 + tgi * 512, 128 + (tgi + 1) * 512
                        qpi = (4 + tgi) if OLDQ else 5
                        for hc in range(HC):
                            P.op("tensor", lambda t, cg=cg, hc=hc, lo=lo, hi=hi, par=par, qpi=qpi:
                                 t.matmul(PB[qpi][:, 0:512], wset[("q", par)][:, hc, cg * 128:(cg + 1) * 128],
                                          x_bf[:, hc, lo:hi], start=(hc == 0), stop=(hc == HC - 1)),
                                 reads=[f"wq{par}", "x_bf"], writes=[f"P{qpi}"])
                        P.op("scalar", lambda sc, cg=cg, tgi=tgi, qpi=qpi:
                             sc.activation(q_b[:, cg, tgi * 512:(tgi + 1) * 512], PB[qpi][:, 0:512], AF.Silu),
                             reads=[f"P{qpi}"], writes=[f"qb{tgi}"])
                    QUNITS.append(qgrp)
            for tb in range(NTB):
                def igrp(tb=tb):
                    for half in range(2):
                        for hc in range(HC):
                            P.op("tensor", lambda t, tb=tb, half=half, hc=hc, par=par:
                                 t.matmul(PB[2][0:64, half * 256:(half + 1) * 256],
                                          x_bf[:, hc, tb * 128 + half * 64: tb * 128 + (half + 1) * 64],
                                          wset[("i", par)][:, hc, :],
                                          start=(hc == 0), stop=(hc == HC - 1)),
                                 reads=[f"wi{par}", "x_bf"], writes=[f"P2{half}"])
                    for half in range(2):
                        P.op("vector", lambda v, tb=tb, half=half:
                             v.tensor_copy(vpool[0:64, 2 * tb + half, :],
                                           PB[2][0:64, half * 256:(half + 1) * 256]),
                             reads=["P20", "P21"], writes=[f"vp{2 * tb + half}"])
                units.append(igrp)
            units.extend(QUNITS)
            return units

        def phase_b(si):
            tiles = [(cg, tgi) for cg in range(2) for tgi in range(3)]
            for cg, tgi in tiles:
                lo, hi = fslice(tgi)
                P.op("scalar", lambda sc, cg=cg, lo=lo, hi=hi:
                     sc.activation(k_b[:, cg, lo:hi], g_f[:, cg, lo:hi], AF.Sigmoid, scale=-1.0),
                     reads=["g_f"], writes=["k_b"])
            for cg, tgi in tiles:
                lo, hi = fslice(tgi)
                P.op("scalar", lambda sc, cg=cg, lo=lo, hi=hi:
                     sc.activation(tA[:, 0:hi - lo], g_f[:, cg, lo:hi], AF.Abs),
                     reads=["g_f"], writes=["tA"])
                P.op("scalar", lambda sc, lo=lo, hi=hi:
                     sc.activation(tA[:, 0:hi - lo], tA[:, 0:hi - lo], AF.Exp, scale=-1.0),
                     reads=["tA"], writes=["tA"])
                P.op("scalar", lambda sc, lo=lo, hi=hi:
                     sc.activation(tA[:, 0:hi - lo], tA[:, 0:hi - lo], AF.Ln, bias=1.0),
                     reads=["tA"], writes=["tA"])
                P.op("vector", lambda v, cg=cg, lo=lo, hi=hi:
                     v.tensor_scalar_min(g_f[:, cg, lo:hi], g_f[:, cg, lo:hi], 0.0),
                     reads=["g_f"], writes=["g_f"])
                P.op("vector", lambda v, cg=cg, lo=lo, hi=hi:
                     v.tensor_sub(g_f[:, cg, lo:hi], g_f[:, cg, lo:hi], tA[:, 0:hi - lo]),
                     reads=["g_f", "tA"], writes=["g_f"])
            P.op("vector", lambda v: v.tensor_tensor_scan(G2d, smask_s[:, :], g2d, 0.0, ALU.mult, ALU.add),
                 reads=["g_f", "smask"], writes=["G_f"])
            P.op("scalar", lambda sc: sc.activation(lq2d, G2d, AF.Exp, bias=cb[:, 0:1]),
                 reads=["G_f", "cb"], writes=["lq_b"])
            P.op("scalar", lambda sc: sc.activation(lk2d, G2d, AF.Exp, scale=-1.0),
                 reads=["G_f"], writes=["lk_b"])
            P.op("vector", lambda v: v.tensor_mul(k2d, k2d, lk2d),
                 reads=["k_b", "lk_b"], writes=["k_b"])
            for tgi in range(2):
                P.op("vector", lambda v, tgi=tgi: v.tensor_mul(
                        q_b[:, :, tgi * 512:(tgi + 1) * 512], q_b[:, :, tgi * 512:(tgi + 1) * 512],
                        lq_b[:, :, HALO + tgi * 512:HALO + (tgi + 1) * 512]),
                     reads=[f"qb{tgi}", "lq_b"], writes=[f"qb{tgi}"])
            P.op("vector", lambda v: v.memset(Hst[:, :], 0.0), writes=["Hst"])
            P.op("vector", lambda v: v.memset(Hb[:, :], 0.0), writes=["Hb"])

        def chunk_loop(si, next_units):
            first_guard = {0: ["xstg0"], 2: ["xstg1"], 4: ["xtmp"]}.get(si, [])
            sched = {}
            tail_units = []
            for k, u in enumerate(next_units):
                if k < 6:
                    tk = k
                elif k < 15:
                    tb = k - 6
                    tk = max(6 + tb, 2 * tb + 3)
                else:
                    tk = NCH + 10
                if tk <= NCH:
                    sched.setdefault(tk, []).append(u)
                else:
                    tail_units.append(u)

            def rec_s(ch):
                if ch < 2:
                    return
                t0 = ch * C
                ot = t0 - HALO
                tp = ch % 2
                for hd in range(2):
                    so = (hd * 2 + tp) * 64
                    P.op("tensor", lambda t, hd=hd, t0=t0, ot=ot:
                         t.matmul(PB[3][0:64, hd * 64:(hd + 1) * 64], k_b[:, hd, t0:t0 + 64],
                                  q_b[:, hd, ot:ot + 64], start=True, stop=True),
                         reads=["k_b", f"qb{min(ot // 512, 1)}"], writes=[f"P3{hd}"])
                    P.op("vector", lambda v, hd=hd, so=so: v.tensor_mul(
                            sbfs[:, so:so + 64], PB[3][0:64, hd * 64:(hd + 1) * 64], triu_s[:, :]),
                         reads=[f"P3{hd}", "triu"], writes=[f"sbf{hd}{tp}"])

            def rec_o(ch):
                t0 = ch * C
                ot = t0 - HALO
                tp = ch % 2
                for hd in range(2):
                    if ch >= 2:
                        so = (hd * 2 + tp) * 64
                        P.op("tensor", lambda t, ch=ch, hd=hd, so=so:
                             t.matmul(PB[0][:, hd * 64:(hd + 1) * 64], vpool[0:64, ch, hd * 128:(hd + 1) * 128],
                                      sbfs[:, so:so + 64], start=True, stop=False),
                             reads=[f"vp{ch}", f"sbf{hd}{tp}"], writes=[f"P0{hd}"])
                        P.op("tensor", lambda t, hd=hd, ot=ot:
                             t.matmul(PB[0][:, hd * 64:(hd + 1) * 64], Hb[:, hd * 128:(hd + 1) * 128],
                                      q_b[:, hd, ot:ot + 64], start=False, stop=True),
                             reads=["Hb", f"qb{min(ot // 512, 1)}"], writes=[f"P0{hd}"])
                    P.op("vector", lambda v, hd=hd, ch=ch, t0=t0:
                         v.tensor_scalar(lame[:, hd * NCH + ch:hd * NCH + ch + 1],
                                         lq_b[:, hd, t0 + 63:t0 + 64], SQRT_DK, None, ALU.mult),
                         reads=["lq_b"], writes=["lame"])
                    P.op("vector", lambda v, hd=hd, ch=ch, t0=t0:
                         v.tensor_scalar(khs[:, hd * 64:(hd + 1) * 64], k_b[:, hd, t0:t0 + 64],
                                         lame[:, hd * NCH + ch:hd * NCH + ch + 1], None, ALU.mult),
                         reads=["k_b", "lame"], writes=[f"khs{hd}"])
                    P.op("tensor", lambda t, hd=hd: t.transpose(
                            PBF[0:64, hd * 128:(hd + 1) * 128], khs[:, hd * 64:(hd + 1) * 64], idbf_s[:, :]),
                         reads=[f"khs{hd}", "idbf"], writes=[f"PBF{hd}"])
                for hd in range(2):
                    gh = 2 * si + hd
                    if ch >= 2:
                        g2 = first_guard if (ch == 2 and hd == 0) else []
                        P.op("vector", lambda v, gh=gh, hd=hd, ot=ot:
                             v.tensor_copy(o_bf[:, gh, ot:ot + 64], PB[0][:, hd * 64:(hd + 1) * 64]),
                             reads=["P00", "P01"], writes=["o_bf"] + g2)
                    P.op("scalar", lambda sc, hd=hd: sc.copy(
                            khts[:, hd * 128:(hd + 1) * 128], PBF[0:64, hd * 128:(hd + 1) * 128]),
                         reads=[f"PBF{hd}"], writes=[f"khts{hd}"])
                    P.op("tensor", lambda t, ch=ch, hd=hd:
                         t.matmul(PB[6][:, hd * 128:(hd + 1) * 128], khts[:, hd * 128:(hd + 1) * 128],
                                  vpool[0:64, ch, hd * 128:(hd + 1) * 128],
                                  start=True, stop=True),
                         reads=[f"khts{hd}", f"vp{ch}"], writes=[f"P6{hd}"])
                    P.op("vector", lambda v, hd=hd, ch=ch:
                         v.scalar_tensor_tensor(Hst[:, hd * 128:(hd + 1) * 128], Hst[:, hd * 128:(hd + 1) * 128],
                                                lame[:, hd * NCH + ch:hd * NCH + ch + 1],
                                                PB[6][:, hd * 128:(hd + 1) * 128], ALU.mult, ALU.add),
                         reads=["Hst", "lame", f"P6{hd}"], writes=["Hst"])
                    P.op("scalar", lambda sc, hd=hd:
                         sc.copy(Hb[:, hd * 128:(hd + 1) * 128], Hst[:, hd * 128:(hd + 1) * 128]),
                         reads=["Hst"], writes=["Hb"])

            for tick in range(NCH + 1):
                if tick < NCH:
                    rec_s(tick)
                if tick >= 1:
                    rec_o(tick - 1)
                for u in sched.get(tick, []):
                    u()
                mat_pump(2)
            for u in tail_units:
                u()

        wload(0)
        wload(1)
        mat_pump(4)
        for u in proj_units(0):
            u()
            mat_pump(1)
        phase_b(0)
        for si in range(8):
            if si + 2 <= 7:
                wload(si + 2)
            nxt = proj_units(si + 1) if si + 1 <= 7 else []
            if os.environ.get('BK_NOIL'):
                pass_units, nxt = nxt, []
            else:
                pass_units = []
            chunk_loop(si, nxt)
            for u in pass_units:
                u()
            if si + 1 <= 7:
                phase_b(si + 1)
        while True:
            f_ = next(mat_it, None)
            if f_ is None:
                break
            f_()

        P.new_block()

        wo_s = cvb(0, HC * 2048).rearrange("p (h c) -> p h c", c=2048)
        xr = [cvf(65536, 512), cvf(67584, 512)]
        x1t = [cvf(69632, 512), cvf(71680, 512)]
        cst_b = [cvb(73728, 512), cvb(74752, 512)]
        sq_b = [cvb(75776, 512), cvb(76800, 512)]

        for c4 in range(4):
            src = wo_d.ap()[:, c4 * 512:(c4 + 1) * 512].rearrange("(h p) c -> p h c", p=128)
            P.op("gpsimd", lambda g, src=src, c4=c4: g.dma_start(wo_s[:, :, c4 * 512:(c4 + 1) * 512], src),
                 writes=[f"wo_s{c4}"], dma=True, dsem="wo")

        for tg in range(2):
            lo = tg * 512
            for hc in range(NH):
                P.op("scalar", lambda sc, hc=hc, lo=lo, tg=tg:
                     sc.activation(sq_b[tg][:, :], o_bf[:, hc, lo:lo + 512], AF.Square),
                     reads=["o_bf"], writes=[f"sqb{tg}"])
                P.op("tensor", lambda t, hc=hc, tg=tg:
                     t.matmul(PB[4][0:1, 0:512], onesb_s[:, :], sq_b[tg][:, :],
                              start=(hc == 0), stop=(hc == NH - 1)),
                     reads=[f"sqb{tg}", "onesb"], writes=["P4"])
            P.op("scalar", lambda sc, lo=lo:
                 sc.activation(rows[0:1, lo:lo + 512], PB[4][0:1, 0:512], AF.Sqrt,
                               scale=1.0 / 2048.0, bias=cb[0:1, 1:2]),
                 reads=["P4", "cb"], writes=["rows"])
            P.op("vector", lambda v, lo=lo: v.reciprocal(rows[0:1, 1024 + lo:1024 + lo + 512],
                                                         rows[0:1, lo:lo + 512]),
                 reads=["rows"], writes=["rows"])
            P.op("tensor", lambda t, lo=lo:
                 t.matmul(PB[2][:, 0:512], ones1_s[:, :], rows[0:1, 1024 + lo:1024 + lo + 512],
                          start=True, stop=True),
                 reads=["rows", "ones1"], writes=["P2"])
            P.op("vector", lambda v, lo=lo: v.tensor_copy(rstdr[:, lo:lo + 512], PB[2][:, 0:512]),
                 reads=["P2"], writes=["rstdr"])
        for hc in range(NH):
            P.op("vector", lambda v, hc=hc:
                 v.scalar_tensor_tensor(o_bf[:, hc, :], o_bf[:, hc, :], gwc_s[:, hc:hc + 1],
                                        rstdr[:, :], ALU.mult, ALU.mult),
                 reads=["o_bf", "gwc", "rstdr"], writes=["o_bf"])

        for tg in range(2):
            lo = tg * 512
            pend = []
            for oc in range(HC):
                par = oc % 2
                for hc in range(HC):
                    P.op("tensor", lambda t, oc=oc, hc=hc, lo=lo, par=par:
                         t.matmul(PB[par][:, 0:512], wo_s[:, hc, oc * 128:(oc + 1) * 128],
                                  o_bf[:, hc, lo:lo + 512], start=(hc == 0), stop=(hc == HC - 1)),
                         reads=[f"wo_s{oc // 4}", "o_bf"], writes=[f"P{par}"])
                for f_ in pend:
                    f_()
                pend = []
                P.op("sync", lambda s, oc=oc, lo=lo, par=par:
                     s.dma_start(xr[par][:, :], xfm_d.ap()[:, oc, HALO + lo:HALO + lo + 512]),
                     reads=["xfmd"], writes=[f"xr{par}"], dma=True, dsem=f"xr{par}")
                P.op("vector", lambda v, par=par:
                     v.tensor_add(x1t[par][:, :], PB[par][:, 0:512], xr[par][:, :]),
                     reads=[f"P{par}", f"xr{par}"], writes=[f"x1t{par}"])
                P.op("sync", lambda s, oc=oc, lo=lo, par=par:
                     s.dma_start(x1p_d.ap()[:, oc, lo:lo + 512], x1t[par][:, :]),
                     reads=[f"x1t{par}"], writes=["x1pd"], dma=True, dsem="x1p")
                P.op("vector", lambda v, par=par: v.tensor_copy(cst_b[par][:, :], x1t[par][:, :]),
                     reads=[f"x1t{par}"], writes=[f"cst{par}"])
                P.op("scalar", lambda sc, par=par: sc.activation(sq_b[par][:, :], cst_b[par][:, :], AF.Square),
                     reads=[f"cst{par}"], writes=[f"sqb{par}"])
                def stats(oc=oc, par=par):
                    P.op("tensor", lambda t, oc=oc, par=par:
                         t.matmul(PB[2][0:1, 0:512], onesb_s[:, :], cst_b[par][:, :],
                                  start=(oc == 0), stop=(oc == HC - 1)),
                         reads=[f"cst{par}", "onesb"], writes=["P2"])
                    P.op("tensor", lambda t, oc=oc, par=par:
                         t.matmul(PB[3][0:1, 0:512], onesb_s[:, :], sq_b[par][:, :],
                                  start=(oc == 0), stop=(oc == HC - 1)),
                         reads=[f"sqb{par}", "onesb"], writes=["P3"])
                pend.append(stats)
            for f_ in pend:
                f_()
            P.op("vector", lambda v: v.tensor_scalar(rows[0:1, 0:512], PB[2][0:1, 0:512],
                                                     1.0 / 2048.0, None, ALU.mult),
                 reads=["P2"], writes=["rows"])
            P.op("vector", lambda v: v.tensor_scalar(rows[0:1, 512:1024], PB[3][0:1, 0:512],
                                                     1.0 / 2048.0, None, ALU.mult),
                 reads=["P3"], writes=["rows"])
            P.op("vector", lambda v: v.tensor_mul(rows[0:1, 1024:1536], rows[0:1, 0:512], rows[0:1, 0:512]),
                 reads=["rows"], writes=["rows"])
            P.op("vector", lambda v: v.tensor_sub(rows[0:1, 512:1024], rows[0:1, 512:1024],
                                                  rows[0:1, 1024:1536]),
                 reads=["rows"], writes=["rows"])
            P.op("scalar", lambda sc: sc.activation(rows[0:1, 1024:1536], rows[0:1, 512:1024],
                                                    AF.Sqrt, bias=cb[0:1, 1:2]),
                 reads=["rows", "cb"], writes=["rows"])
            P.op("vector", lambda v: v.reciprocal(rows[0:1, 1536:2048], rows[0:1, 1024:1536]),
                 reads=["rows"], writes=["rows"])
            P.op("tensor", lambda t: t.matmul(PB[2][:, 0:512], ones1_s[:, :], rows[0:1, 0:512],
                                              start=True, stop=True),
                 reads=["rows", "ones1"], writes=["P2"])
            P.op("tensor", lambda t: t.matmul(PB[3][:, 0:512], ones1_s[:, :], rows[0:1, 1536:2048],
                                              start=True, stop=True),
                 reads=["rows", "ones1"], writes=["P3"])
            for oc in range(HC):
                par = oc % 2
                P.op("sync", lambda s, oc=oc, lo=lo, par=par:
                     s.dma_start(xr[par][:, :], x1p_d.ap()[:, oc, lo:lo + 512]),
                     reads=["x1pd"], writes=[f"xr{par}"], dma=True, dsem=f"x1r{par}")
                P.op("vector", lambda v, par=par: v.tensor_sub(x1t[par][:, :], xr[par][:, :], PB[2][:, 0:512]),
                     reads=[f"xr{par}", "P2"], writes=[f"x1t{par}"])
                P.op("vector", lambda v, par=par: v.tensor_mul(x1t[par][:, :], x1t[par][:, :], PB[3][:, 0:512]),
                     reads=[f"x1t{par}", "P3"], writes=[f"x1t{par}"])
                P.op("sync", lambda s, oc=oc, lo=lo, par=par:
                     s.dma_start(x1n_d.ap()[:, oc, lo:lo + 512], x1t[par][:, :]),
                     reads=[f"x1t{par}"], writes=["x1nd"], dma=True, dsem="x1n")
                P.op("vector", lambda v, oc=oc, lo=lo, par=par:
                     v.tensor_copy(x1_bf[:, oc, lo:lo + 512], x1t[par][:, :]),
                     reads=[f"x1t{par}"], writes=["x1_bf"])

        P.new_block()

        slabh = [cvb(94208, 8 * 768).rearrange("p (h c) -> p h c", c=384),
                 cvb(106496, 8 * 768).rearrange("p (h c) -> p h c", c=384)]
        gsil = cvb(159744, GOB_PER * TOWN).rearrange("p (j t) -> p j t", t=TOWN)

        def slab_load(cc_o_t, r, half, tag):
            src = cc_o_t.ap()[r * H:(r + 1) * H, half * 384:(half + 1) * 384] \
                .rearrange("(c p) o -> p c o", p=128)
            P.op("sync", lambda s, src=src, half=half: s.dma_start(slabh[half][:, :, :], src),
                 reads=[tag], writes=[f"slab{half}"], dma=True, dsem=f"sl{half}")

        for r in range(8):
            nj = min(GOB_PER, NOB - r * GOB_PER)
            for half in range(2):
                slab_load(ccg_o, r, half, "ccg_o")
                for j in range(half * 3, min(nj, half * 3 + 3)):
                    for tg in range(2):
                        pi = 4 + tg
                        for hc in range(HC):
                            P.op("tensor", lambda t, j=j, hc=hc, tg=tg, pi=pi, half=half:
                                 t.matmul(PB[pi][:, 0:512], slabh[half][:, hc, (j % 3) * 128:(j % 3 + 1) * 128],
                                          x1_bf[:, hc, tg * 512:(tg + 1) * 512],
                                          start=(hc == 0), stop=(hc == HC - 1)),
                                 reads=[f"slab{half}", "x1_bf"], writes=[f"P{pi}"])
                        P.op("scalar", lambda sc, j=j, tg=tg, pi=pi:
                             sc.activation(gsil[:, j, tg * 512:(tg + 1) * 512], PB[pi][:, 0:512], AF.Silu),
                             reads=[f"P{pi}"], writes=["gsil"])
            for half in range(2):
                slab_load(ccu_o, r, half, "ccu_o")
                for j in range(half * 3, min(nj, half * 3 + 3)):
                    ob = r * GOB_PER + j
                    for tg in range(2):
                        par = tg
                        for hc in range(HC):
                            P.op("tensor", lambda t, j=j, hc=hc, tg=tg, par=par, half=half:
                                 t.matmul(PB[par][:, 0:512], slabh[half][:, hc, (j % 3) * 128:(j % 3 + 1) * 128],
                                          x1_bf[:, hc, tg * 512:(tg + 1) * 512],
                                          start=(hc == 0), stop=(hc == HC - 1)),
                                 reads=[f"slab{half}", "x1_bf"], writes=[f"P{par}"])
                        P.op("vector", lambda v, ob=ob, j=j, tg=tg, par=par:
                             v.tensor_mul(h_bf[:, ob, tg * 512:(tg + 1) * 512],
                                          gsil[:, j, tg * 512:(tg + 1) * 512], PB[par][:, 0:512]),
                             reads=["gsil", f"P{par}"], writes=["h_bf"])

        P.new_block()

        wd_s = [cvb(126976, NOB * 256).rearrange("p (c o) -> p c o", o=256),
                cvb(94208, NOB * 256).rearrange("p (c o) -> p c o", o=256)]
        x1r5 = [cvf(149504, 512), cvf(151552, 512), cvf(153600, 512), cvf(155648, 512)]
        x2t5 = [cvf(157696, 512), cvf(159744, 512)]
        c5_b = [cvb(161792, 512), cvb(162816, 512)]
        s5_b = [cvb(163840, 512), cvb(164864, 512)]
        outt = cvf(165888, 2048)
        xnrm = [cvf(174080, 128), cvf(174592, 128)]

        pend5 = []
        it5 = 0
        for r in range(8):
            wp = r % 2
            srcd = ccd_o.ap()[r * INTER:(r + 1) * INTER, :].rearrange("(c p) o -> p c o", p=128)
            P.op("sync", lambda s, srcd=srcd, wp=wp: s.dma_start(wd_s[wp][:, :, :], srcd),
                 reads=["ccd_o"], writes=[f"wd{wp}"], dma=True, dsem=f"wd{wp}")
            for j in range(DOB_PER):
                ob = r * DOB_PER + j
                for tg in range(2):
                    par = tg
                    lo = tg * 512
                    lp = it5 % 4
                    it5 += 1
                    P.op("sync", lambda s, ob=ob, lo=lo, lp=lp:
                         s.dma_start(x1r5[lp][:, :], x1n_d.ap()[:, ob, lo:lo + 512]),
                         reads=["x1nd"], writes=[f"x1r5{lp}"], dma=True, dsem=f"x2l{lp}")
                    for ic in range(NOB):
                        P.op("tensor", lambda t, j=j, ic=ic, tg=tg, par=par, wp=wp:
                             t.matmul(PB[par][:, 0:512], wd_s[wp][:, ic, j * 128:(j + 1) * 128],
                                      h_bf[:, ic, tg * 512:(tg + 1) * 512],
                                      start=(ic == 0), stop=(ic == NOB - 1)),
                             reads=[f"wd{wp}", "h_bf"], writes=[f"P{par}"])
                    for f_ in pend5:
                        f_()
                    pend5 = []
                    P.op("vector", lambda v, par=par, lp=lp:
                         v.tensor_add(x2t5[par][:, :], PB[par][:, 0:512], x1r5[lp][:, :]),
                         reads=[f"P{par}", f"x1r5{lp}"], writes=[f"x2t{par}"])
                    P.op("sync", lambda s, ob=ob, lo=lo, par=par:
                         s.dma_start(x2p_d.ap()[:, ob, lo:lo + 512], x2t5[par][:, :]),
                         reads=[f"x2t{par}"], writes=["x2pd"], dma=True, dsem="x2p")
                    P.op("vector", lambda v, par=par: v.tensor_copy(c5_b[par][:, :], x2t5[par][:, :]),
                         reads=[f"x2t{par}"], writes=[f"c5{par}"])
                    P.op("scalar", lambda sc, par=par: sc.activation(s5_b[par][:, :], c5_b[par][:, :], AF.Square),
                         reads=[f"c5{par}"], writes=[f"s5{par}"])
                    def stats5(ob=ob, tg=tg, par=par):
                        P.op("tensor", lambda t, ob=ob, tg=tg, par=par:
                             t.matmul(PB[2][32 * tg:32 * tg + 1, 0:512], onesb_s[:, :], c5_b[par][:, :],
                                      start=(ob == 0), stop=(ob == HC - 1)),
                             reads=[f"c5{par}", "onesb"], writes=[f"P2s{tg}"])
                        P.op("tensor", lambda t, ob=ob, tg=tg, par=par:
                             t.matmul(PB[3][32 * tg:32 * tg + 1, 0:512], onesb_s[:, :], s5_b[par][:, :],
                                      start=(ob == 0), stop=(ob == HC - 1)),
                             reads=[f"s5{par}", "onesb"], writes=[f"P3s{tg}"])
                    pend5.append(stats5)
        for f_ in pend5:
            f_()

        for tg in range(2):
            P.op("vector", lambda v, tg=tg: v.tensor_scalar(rows[0:1, tg * 512:tg * 512 + 512],
                                                            PB[2][32 * tg:32 * tg + 1, 0:512],
                                                            1.0 / 2048.0, None, ALU.mult),
                 reads=[f"P2s{tg}"], writes=["rows"])
            P.op("vector", lambda v, tg=tg: v.tensor_scalar(rows[0:1, 1024 + tg * 512:1536 + tg * 512],
                                                            PB[3][32 * tg:32 * tg + 1, 0:512],
                                                            1.0 / 2048.0, None, ALU.mult),
                 reads=[f"P3s{tg}"], writes=["rows"])
            P.op("vector", lambda v, tg=tg: v.tensor_mul(rstdr[0:1, 0:512],
                                                         rows[0:1, tg * 512:tg * 512 + 512],
                                                         rows[0:1, tg * 512:tg * 512 + 512]),
                 reads=["rows"], writes=["rstdr"])
            P.op("vector", lambda v, tg=tg: v.tensor_sub(rows[0:1, 1024 + tg * 512:1536 + tg * 512],
                                                         rows[0:1, 1024 + tg * 512:1536 + tg * 512],
                                                         rstdr[0:1, 0:512]),
                 reads=["rows", "rstdr"], writes=["rows"])
            P.op("scalar", lambda sc, tg=tg: sc.activation(rows[0:1, 1024 + tg * 512:1536 + tg * 512],
                                                           rows[0:1, 1024 + tg * 512:1536 + tg * 512],
                                                           AF.Sqrt, bias=cb[0:1, 1:2]),
                 reads=["rows", "cb"], writes=["rows"])
            P.op("vector", lambda v, tg=tg: v.reciprocal(rstdr[0:1, 512 - tg * 512:1024 - tg * 512],
                                                         rows[0:1, 1024 + tg * 512:1536 + tg * 512]),
                 reads=["rows"], writes=["rstdr"])

        for tg in range(2):
            P.op("tensor", lambda t, tg=tg: t.matmul(PB[2][:, 0:512], ones1_s[:, :],
                                                     rows[0:1, tg * 512:tg * 512 + 512],
                                                     start=True, stop=True),
                 reads=["rows", "ones1"], writes=["P2", "P2s0", "P2s1"])
            P.op("tensor", lambda t, tg=tg: t.matmul(PB[3][:, 0:512], ones1_s[:, :],
                                                     rstdr[0:1, 512 - tg * 512:1024 - tg * 512],
                                                     start=True, stop=True),
                 reads=["rstdr", "ones1"], writes=["P3", "P3s0", "P3s1"])
            for tbq in range(4):
                tb = tg * 4 + tbq
                for oc in range(HC):
                    par = oc % 2
                    P.op("sync", lambda s, oc=oc, tb=tb, par=par:
                         s.dma_start(xnrm[par][:, :], x2p_d.ap()[:, oc, tb * 128:(tb + 1) * 128]),
                         reads=["x2pd"], writes=[f"xn{par}"], dma=True, dsem=f"x2r{par}")
                    P.op("vector", lambda v, par=par, tbq=tbq:
                         v.tensor_sub(xnrm[par][:, :], xnrm[par][:, :],
                                      PB[2][:, tbq * 128:(tbq + 1) * 128]),
                         reads=[f"xn{par}", "P2"], writes=[f"xn{par}"])
                    P.op("vector", lambda v, par=par, tbq=tbq:
                         v.tensor_mul(xnrm[par][:, :], xnrm[par][:, :],
                                      PB[3][:, tbq * 128:(tbq + 1) * 128]),
                         reads=[f"xn{par}", "P3"], writes=[f"xn{par}"])
                    P.op("tensor", lambda t, par=par: t.transpose(PB[6][:, 0:128], xnrm[par][:, :], idf_s[:, :]),
                         reads=[f"xn{par}", "idf"], writes=["P6"])
                    P.op("vector", lambda v, oc=oc: v.tensor_copy(outt[:, oc * 128:(oc + 1) * 128],
                                                                  PB[6][:, 0:128]),
                         reads=["P6"], writes=["outt"])
                P.op("sync", lambda s, tb=tb: s.dma_start(y_d.ap()[tb * 128:(tb + 1) * 128, :], outt[:, :]),
                     reads=["outt"], dma=True, dsem="out")
        outsem = P._dsem("out")
        P.op("sync", lambda s: s.wait_ge(outsem, 16 * 8))

        P.emit()
    nc.finalize()
    return nc


_CACHE = {}


def kernel(L0_attn_x, L0_attn_Wq, L0_attn_Wf, L0_attn_Wi, L0_attn_Wo,
           L0_attn_g_weight, L0_ffn_lut, L0_ffn_gate_idx, L0_ffn_up_idx,
           L0_ffn_down_idx):
    x = np.ascontiguousarray(np.asarray(L0_attn_x, np.float32))
    wq = np.ascontiguousarray(np.asarray(L0_attn_Wq, np.float32))
    wf = np.ascontiguousarray(np.asarray(L0_attn_Wf, np.float32))
    wi = np.ascontiguousarray(np.asarray(L0_attn_Wi, np.float32))
    wo = np.ascontiguousarray(np.asarray(L0_attn_Wo, np.float32))
    gw = np.asarray(L0_attn_g_weight, np.float32)
    lut = np.asarray(L0_ffn_lut, np.float32).reshape(1, 16)
    gidx = np.asarray(L0_ffn_gate_idx, np.int32)
    uidx = np.asarray(L0_ffn_up_idx, np.int32)
    didx = np.asarray(L0_ffn_down_idx, np.int32)

    gwc = np.ascontiguousarray(gw.reshape(16, 128).T)
    gT = np.zeros((H, 48 * 128), np.int32)
    gT[:, :INTER] = gidx.T
    uT = np.zeros((H, 48 * 128), np.int32)
    uT[:, :INTER] = uidx.T
    dT = np.ascontiguousarray(didx.T)

    if "nc" not in _CACHE:
        _CACHE["nc"] = build()
    nc = _CACHE["nc"]

    zero_halo = np.zeros((HALO, H), np.float32)
    in_maps = []
    for c in range(8):
        b, hf = c // 2, c % 2
        m = {
            "x_own": np.ascontiguousarray(x[b, hf * TOWN:(hf + 1) * TOWN]),
            "x_halo": zero_halo if hf == 0 else np.ascontiguousarray(x[b, TOWN - HALO:TOWN]),
            "wq": wq, "wf": wf, "wi": wi, "wo": wo,
            "gwc": gwc, "lut16": lut,
            "gidxs": np.ascontiguousarray(gT[:, c * 768:(c + 1) * 768]),
            "uidxs": np.ascontiguousarray(uT[:, c * 768:(c + 1) * 768]),
            "didxs": np.ascontiguousarray(dT[:, c * 256:(c + 1) * 256]),
        }
        in_maps.append(m)

    res = run_bass_kernel_spmd(nc, in_maps, core_ids=list(range(8)))
    out = np.empty((4, 2048, H), np.float32)
    for c in range(8):
        b, hf = c // 2, c % 2
        out[b, hf * TOWN:(hf + 1) * TOWN] = res.results[c]["y"]
    return out


# version 09
# speedup vs baseline: 1.0218x; 1.0218x over previous
import os
import numpy as np
import ml_dtypes
from contextlib import ExitStack
import concourse.bass as bass
import concourse.mybir as mybir
from concourse.bass_utils import run_bass_kernel_spmd

F32 = mybir.dt.float32
BF16 = mybir.dt.bfloat16
I32 = mybir.dt.int32
AF = mybir.ActivationFunctionType
ALU = mybir.AluOpType

ENGINES = ("sync", "tensor", "vector", "scalar", "gpsimd")

H = 2048
HC = 16
NH = 16
DK = 128
TOWN = 1024
HALO = 128
TT = TOWN + HALO
NTB = TT // 128
C = 64
NCH = TT // C
INTER = 5504
NOB = 43
GOB_PER = 6
DOB_PER = 2
LNC_Q = float(-0.5 * np.log(128.0))
SQRT_DK = float(np.sqrt(128.0))
EPS = 1e-5


class Prog:

    def __init__(self, nc, st):
        self.nc = nc
        self.st = st
        self.chain = {e: st.enter_context(nc.semaphore(f"ch_{e}")) for e in ENGINES}
        self.dsems = {}
        self.blocks = [[]]
        self.res = {}
        self.ops = []
        self.dtot = {}
        self.fenced = {}

    def _dsem(self, key):
        if key not in self.dsems:
            self.dsems[key] = self.st.enter_context(self.nc.semaphore(f"dq_{key}"))
        return self.dsems[key]

    def _key(self, item):
        return ("r", item)

    def op(self, eng, fn, reads=(), writes=(), dma=False, dsem=None, asyn=False):
        idx = len(self.ops)
        deps = set()
        rkeys = [self._key(r) for r in reads]
        wkeys = [self._key(w) for w in writes]
        for k in rkeys:
            v = self.res.get(k)
            if v and v["w"] is not None:
                deps.add(v["w"])
        for k in wkeys:
            v = self.res.get(k)
            if v:
                if v["w"] is not None:
                    deps.add(v["w"])
                deps.update(v["r"])
        o = dict(eng=eng, fn=fn, dma=dma, deps=deps, idx=idx, waited=False,
                 cnt=None, blk=len(self.blocks) - 1, asyn=asyn or dma,
                 dsem=(dsem or f"_dq_{eng}") if dma else None)
        if dma:
            self.dtot[o["dsem"]] = self.dtot.get(o["dsem"], 0) + 16
        self.ops.append(o)
        self.blocks[-1].append(o)
        for k in rkeys:
            self.res.setdefault(k, {"w": None, "r": []})["r"].append(idx)
        for k in wkeys:
            e = self.res.setdefault(k, {"w": None, "r": []})
            e["w"] = idx
            e["r"] = []
        return idx

    def new_block(self):
        pend = {k: v for k, v in self.dtot.items() if v > self.fenced.get(k, 0)}
        if pend:
            def fence(s, pend=dict(pend)):
                inst = None
                for k, v in pend.items():
                    inst = s.wait_ge(self._dsem(k), v)
                return inst
            self.op("sync", fence)
            self.fenced.update(pend)
        if self.blocks[-1]:
            self.blocks.append([])

    def emit(self):
        for o in self.ops:
            if o["dma"]:
                o["waited"] = True
            real = set()
            for d in o["deps"]:
                p = self.ops[d]
                if p["blk"] < o["blk"] and not p["asyn"]:
                    continue
                if (p["eng"] == "tensor" and o["eng"] == "tensor"
                        and not p["dma"] and not o["dma"]):
                    continue
                real.add(d)
                p["waited"] = True
            o["rdeps"] = real
        cnt = {e: 0 for e in ENGINES}
        dcnt = {}
        for o in self.ops:
            if o["dma"]:
                dcnt[o["dsem"]] = dcnt.get(o["dsem"], 0) + 16
                o["cnt"] = dcnt[o["dsem"]]
            elif o["waited"]:
                cnt[o["eng"]] += 1
                o["cnt"] = cnt[o["eng"]]
        run = {}
        for o in self.ops:
            o["_run"] = dict(run)
            if o["dma"]:
                run[o["dsem"]] = run.get(o["dsem"], 0) + 16
        for blk in self.blocks:
            hi = {}
            for o in blk:
                o["waits"] = []
                for d in sorted(o["rdeps"]):
                    p = self.ops[d]
                    if p["dma"]:
                        semkey = ("d", p["dsem"])
                        val = o["_run"].get(p["dsem"], p["cnt"])
                        sem = self._dsem(p["dsem"])
                    else:
                        semkey = ("c", p["eng"])
                        val = p["cnt"]
                        sem = self.chain[p["eng"]]
                    key = (o["eng"], semkey)
                    if hi.get(key, 0) >= val:
                        continue
                    hi[key] = val
                    o["waits"].append((sem, val))
            byeng = {}
            for o in blk:
                byeng.setdefault(o["eng"], []).append(o)
            with self.nc.Block() as block:
                for eng, ops in byeng.items():
                    def body(engine, ops=ops):
                        for o in ops:
                            for sem, val in o["waits"]:
                                engine.wait_ge(sem, val)
                            inst = o["fn"](engine)
                            if o["dma"]:
                                inst.then_inc(self._dsem(o["dsem"]), 16)
                            elif o["waited"]:
                                inst.then_inc(self.chain[o["eng"]], 1)
                    getattr(block, eng)(body)


def build():
    nc = bass.Bass("TRN2", target_bir_lowering=False)

    x_own = nc.dram_tensor("x_own", [TOWN, H], F32, kind="ExternalInput")
    x_halo = nc.dram_tensor("x_halo", [HALO, H], F32, kind="ExternalInput")
    wq_d = nc.dram_tensor("wq", [H, H], F32, kind="ExternalInput")
    wf_d = nc.dram_tensor("wf", [H, H], F32, kind="ExternalInput")
    wi_d = nc.dram_tensor("wi", [H, H], F32, kind="ExternalInput")
    wo_d = nc.dram_tensor("wo", [H, H], F32, kind="ExternalInput")
    gwc_d = nc.dram_tensor("gwc", [128, 16], F32, kind="ExternalInput")
    lut_d = nc.dram_tensor("lut16", [1, 16], F32, kind="ExternalInput")
    gidx_d = nc.dram_tensor("gidxs", [H, GOB_PER * 128], I32, kind="ExternalInput")
    uidx_d = nc.dram_tensor("uidxs", [H, GOB_PER * 128], I32, kind="ExternalInput")
    didx_d = nc.dram_tensor("didxs", [INTER, DOB_PER * 128], I32, kind="ExternalInput")
    y_d = nc.dram_tensor("y", [TOWN, H], F32, kind="ExternalOutput")

    ccg_i = nc.dram_tensor("ccg_i", [H, GOB_PER * 128], BF16)
    ccu_i = nc.dram_tensor("ccu_i", [H, GOB_PER * 128], BF16)
    ccd_i = nc.dram_tensor("ccd_i", [INTER, DOB_PER * 128], BF16)
    ccg_o = nc.dram_tensor("ccg_o", [8 * H, GOB_PER * 128], BF16)
    ccu_o = nc.dram_tensor("ccu_o", [8 * H, GOB_PER * 128], BF16)
    ccd_o = nc.dram_tensor("ccd_o", [8 * INTER, DOB_PER * 128], BF16)
    xfm_d = nc.dram_tensor("xfm_d", [128, HC, TT], F32)
    x1p_d = nc.dram_tensor("x1p_d", [128, HC, TOWN], F32)
    x1n_d = nc.dram_tensor("x1n_d", [128, HC, TOWN], F32)
    x2p_d = nc.dram_tensor("x2p_d", [128, HC, TOWN], F32)

    ident_bf = nc.inline_tensor(np.eye(128, dtype=ml_dtypes.bfloat16), name="idbf")
    ident_f = nc.inline_tensor(np.eye(128, dtype=np.float32), name="idf")
    triu_c = nc.inline_tensor(np.triu(np.ones((64, 64), np.float32)), name="triu")
    onesb_c = nc.inline_tensor(np.ones((128, 1), ml_dtypes.bfloat16), name="onesb")
    ones1_c = nc.inline_tensor(np.ones((1, 128), np.float32), name="ones1")

    with ExitStack() as st:
        e = st.enter_context
        P = Prog(nc, st)

        AR_COLS = 44032
        arena = e(nc.sbuf_tensor("arena", [128, AR_COLS], F32))
        arf = arena.ap()
        arb = arena.ap().bitcast(BF16)

        def cvf(off_b, n):
            assert off_b % 4 == 0
            return arf[:, off_b // 4: off_b // 4 + n]

        def cvb(off_b, n):
            assert off_b % 2 == 0
            return arb[:, off_b // 2: off_b // 2 + n]

        idbf_s = e(nc.sbuf_tensor("idbf_s", [128, 128], BF16))
        idf_s = e(nc.sbuf_tensor("idf_s", [128, 128], F32))
        triu_s = e(nc.sbuf_tensor("triu_s", [64, 64], F32))
        onesb_s = e(nc.sbuf_tensor("onesb_s", [128, 1], BF16))
        ones1_s = e(nc.sbuf_tensor("ones1_s", [1, 128], F32))
        gwc_s = e(nc.sbuf_tensor("gwc_s", [128, 16], F32))
        lrow_s = e(nc.sbuf_tensor("lrow_s", [1, 16], F32))
        lrep_s = e(nc.sbuf_tensor("lrep_s", [128, 16], F32))
        smask_s = e(nc.sbuf_tensor("smask_s", [128, 2304], F32))
        Hst = e(nc.sbuf_tensor("Hst", [128, 256], F32))
        Hb = e(nc.sbuf_tensor("Hb", [128, 256], BF16))
        khs = e(nc.sbuf_tensor("khs", [128, 128], BF16))
        khts = e(nc.sbuf_tensor("khts", [64, 256], BF16))
        sbfs = e(nc.sbuf_tensor("sbfs", [64, 256], BF16))
        lame = e(nc.sbuf_tensor("lame", [128, 36], F32))
        rows = e(nc.sbuf_tensor("rows", [1, 2048], F32))
        rstdr = e(nc.sbuf_tensor("rstdr", [128, 1024], F32))
        cb = e(nc.sbuf_tensor("cb", [128, 2], F32))
        mat_i = e(nc.sbuf_tensor("mat_i", [128, 1536], BF16))
        mat_a = e(nc.sbuf_tensor("mat_a", [128, 1536], BF16))
        mat_t = e(nc.sbuf_tensor("mat_t", [128, 1536], BF16))

        PB = [e(nc.psum_tensor(f"pb{i}", [128, 512], F32)) for i in range(7)]
        PBF = e(nc.psum_tensor("pbf", [128, 1024], BF16))

        o_bf = cvb(94208, NH * TOWN).rearrange("p (h t) -> p h t", t=TOWN)
        x1_bf = cvb(126976, HC * TOWN).rearrange("p (h t) -> p h t", t=TOWN)
        h_bf = cvb(0, NOB * TOWN).rearrange("p (h t) -> p h t", t=TOWN)
        x_bf = cvb(0, HC * TT).rearrange("p (h t) -> p h t", t=TT)
        xstg = [cvf(94208, 2048), cvf(102400, 2048)]
        xtmp = cvf(110592, 2048)

        wset = {}
        for i, w in enumerate(("f", "q", "i")):
            for par in range(2):
                off = 45056 + (i * 2 + par) * 8192
                wset[(w, par)] = cvb(off, HC * 256).rearrange("p (h c) -> p h c", c=256)
        khall = cvb(36864, 2 * TT).rearrange("p (h t) -> p h t", t=TT)
        g_f = cvf(126976, 2 * TT).rearrange("p (h t) -> p h t", t=TT)
        g2d = cvf(126976, 2 * TT)
        G_f = cvf(136192, 2 * TT).rearrange("p (h t) -> p h t", t=TT)
        G2d = cvf(136192, 2 * TT)
        q_b = cvb(145408, 2 * TOWN).rearrange("p (h t) -> p h t", t=TOWN)
        vpool = cvb(149504, NCH * 256).rearrange("p (c v) -> p c v", v=256)
        lq_b = cvb(159744, 2 * TT).rearrange("p (h t) -> p h t", t=TT)
        lk_b = cvb(164352, 2 * TT).rearrange("p (h t) -> p h t", t=TT)
        k_b = cvb(168960, 2 * TT).rearrange("p (h t) -> p h t", t=TT)
        lq2d = cvb(159744, 2 * TT)
        lk2d = cvb(164352, 2 * TT)
        k2d = cvb(168960, 2 * TT)
        tA = cvf(173568, 512)

        P.op("sync", lambda s: s.dma_start(idbf_s[:, :], ident_bf.ap()), writes=["idbf"], dma=True, dsem="cst")
        P.op("sync", lambda s: s.dma_start(idf_s[:, :], ident_f.ap()), writes=["idf"], dma=True, dsem="cst")
        P.op("sync", lambda s: s.dma_start(triu_s[:, :], triu_c.ap()), writes=["triu"], dma=True, dsem="cst")
        P.op("sync", lambda s: s.dma_start(onesb_s[:, :], onesb_c.ap()), writes=["onesb"], dma=True, dsem="cst")
        P.op("sync", lambda s: s.dma_start(ones1_s[:, :], ones1_c.ap()), writes=["ones1"], dma=True, dsem="cst")
        P.op("sync", lambda s: s.dma_start(gwc_s[:, :], gwc_d.ap()), writes=["gwc"], dma=True, dsem="cst")
        P.op("sync", lambda s: s.dma_start(lrow_s[:, :], lut_d.ap()), writes=["lrow"], dma=True, dsem="cst")
        P.op("tensor", lambda t: t.matmul(PB[6][:, 0:16], ones1_s[:, :], lrow_s[:, :], start=True, stop=True),
             reads=["ones1", "lrow"], writes=["P60"])
        P.op("vector", lambda v: v.tensor_copy(lrep_s[:, :], PB[6][:, 0:16]),
             reads=["P60"], writes=["lrep"])
        P.op("vector", lambda v: v.memset(smask_s[:, :], 1.0), writes=["smask"])
        sm3 = smask_s.ap().rearrange("p (c t) -> p c t", t=64)
        P.op("vector", lambda v: v.memset(sm3[:, :, 0:1], 0.0), reads=["smask"], writes=["smask"])
        P.op("vector", lambda v: v.memset(cb[:, 0:1], LNC_Q), writes=["cb"])
        P.op("vector", lambda v: v.memset(cb[:, 1:2], EPS), reads=["cb"], writes=["cb"])

        for tb in range(NTB):
            par = tb % 2
            src = x_halo.ap() if tb == 0 else x_own.ap()[(tb - 1) * 128: tb * 128, :]
            P.op("sync", lambda s, src=src, par=par: s.dma_start(xstg[par][:, :], src),
                 writes=[f"xstg{par}"], dma=True, dsem=f"xs{par}")
            for hc in range(HC):
                tp = hc % 2 if not os.environ.get("BK_XSEQ") else 0
                P.op("tensor", lambda t, par=par, hc=hc, tp=tp: t.transpose(
                        PB[6][:, tp * 128:(tp + 1) * 128], xstg[par][:, hc * 128:(hc + 1) * 128], idf_s[:, :]),
                     reads=[f"xstg{par}", "idf"], writes=[f"P6{tp}"])
                P.op("vector", lambda v, hc=hc, tp=tp: v.tensor_copy(
                        xtmp[:, hc * 128:(hc + 1) * 128], PB[6][:, tp * 128:(tp + 1) * 128]),
                     reads=[f"P6{tp}"], writes=["xtmp"])
                P.op("scalar", lambda s, hc=hc, tb=tb: s.copy(x_bf[:, hc, tb * 128:(tb + 1) * 128],
                                                              xtmp[:, hc * 128:(hc + 1) * 128]),
                     reads=["xtmp"], writes=["x_bf"])
            P.op("sync", lambda s, tb=tb: s.dma_start(
                    xfm_d.ap()[:, :, tb * 128:(tb + 1) * 128],
                    xtmp.rearrange("p (h t) -> p h t", t=128)),
                 reads=["xtmp"], writes=["xfmd"], dma=True, dsem="xfd")

        def mk_mat_ops(src_d, dst_d, r0, nr, ncols, tagp):
            n = nr * ncols
            src = src_d.ap()[r0 * 128:(r0 + nr) * 128, :].rearrange("(c p) o -> p c o", p=128)
            yield lambda: P.op("gpsimd", lambda g: g.dma_start(
                    mat_i[:, 0:n].rearrange("p (c o) -> p c o", o=ncols), src),
                 writes=["mati"], dma=True, dsem="mi")
            yield lambda: P.op("vector", lambda v: v.tensor_scalar(
                    mat_a[:, 0:n], mat_i[:, 0:n], 0.0, lrep_s[:, 0:1], ALU.is_equal, ALU.mult),
                 reads=["mati", "lrep"], writes=["mata"])
            for j in range(1, 16):
                def two(j=j):
                    P.op("vector", lambda v: v.tensor_scalar(mat_t[:, 0:n], mat_i[:, 0:n], float(j),
                                                             lrep_s[:, j:j + 1], ALU.is_equal, ALU.mult),
                         reads=["mati", "lrep"], writes=["matt"])
                    P.op("vector", lambda v: v.tensor_add(mat_a[:, 0:n], mat_a[:, 0:n], mat_t[:, 0:n]),
                         reads=["mata", "matt"], writes=["mata"])
                yield two
            dst = dst_d.ap()[r0 * 128:(r0 + nr) * 128, :].rearrange("(c p) o -> p c o", p=128)
            yield lambda: P.op("sync", lambda s: s.dma_start(
                    dst, mat_a[:, 0:n].rearrange("p (c o) -> p c o", o=ncols)),
                 reads=["mata"], writes=[tagp + "_w"], dma=True, dsem=tagp)

        def mk_cc_ops(cin, cout, tagp, otag):
            yield lambda: P.op("gpsimd", lambda g: g.collective_compute(
                    "AllGather", ALU.bypass, replica_groups=[list(range(8))],
                    ins=[cin.ap().opt()], outs=[cout.ap().opt()]),
                 reads=[tagp + "_w"], writes=[otag], asyn=True)

        def mat_all():
            for r0 in range(0, HC, 2):
                yield from mk_mat_ops(gidx_d, ccg_i, r0, 2, 768, "ccg")
            yield from mk_cc_ops(ccg_i, ccg_o, "ccg", "ccg_o")
            for r0 in range(0, HC, 2):
                yield from mk_mat_ops(uidx_d, ccu_i, r0, 2, 768, "ccu")
            yield from mk_cc_ops(ccu_i, ccu_o, "ccu", "ccu_o")
            for r0 in range(0, 42, 6):
                yield from mk_mat_ops(didx_d, ccd_i, r0, 6, 256, "ccd")
            yield from mk_mat_ops(didx_d, ccd_i, 42, 1, 256, "ccd")
            yield from mk_cc_ops(ccd_i, ccd_o, "ccd", "ccd_o")

        mat_it = iter(mat_all())

        def mat_pump(k=1):
            for _ in range(k):
                f_ = next(mat_it, None)
                if f_ is None:
                    return
                f_()

        def fslice(tg):
            return (0, 512) if tg == 0 else ((512, 1024) if tg == 1 else (1024, 1152))

        def wload(si):
            par = si % 2
            for w, wd in (("f", wf_d), ("q", wq_d), ("i", wi_d)):
                src = wd.ap()[:, si * 256:(si + 1) * 256].rearrange("(h p) c -> p h c", p=128)
                P.op("gpsimd", lambda g, w=w, par=par, src=src: g.dma_start(wset[(w, par)][:, :, :], src),
                     writes=[f"w{w}{par}"], dma=True, dsem=f"w{w}{par}")

        def proj_units(si):
            par = si % 2
            units = []
            OLDPS = bool(os.environ.get("BK_OLDPS"))
            OLDF = OLDPS or bool(os.environ.get("BK_OLDF"))
            OLDQ = OLDPS or bool(os.environ.get("BK_OLDQ"))
            tiles = [(cg, tgi) for cg in range(2) for tgi in range(3)]
            for ti, (cg, tgi) in enumerate(tiles):
                def fgrp(cg=cg, tgi=tgi, ti=ti):
                    lo, hi = fslice(tgi)
                    pi = 4 + ti % 2
                    pp = PB[pi]
                    wtags = [f"P{pi}"]
                    for hc in range(HC):
                        P.op("tensor", lambda t, cg=cg, hc=hc, lo=lo, hi=hi, pp=pp, par=par:
                             t.matmul(pp[:, 0:hi - lo], wset[("f", par)][:, hc, cg * 128:(cg + 1) * 128],
                                      x_bf[:, hc, lo:hi], start=(hc == 0), stop=(hc == HC - 1)),
                             reads=[f"wf{par}", "x_bf"], writes=wtags)
                    P.op("scalar", lambda sc, cg=cg, lo=lo, hi=hi, pp=pp, pi=pi:
                         sc.copy(g_f[:, cg, lo:hi], pp[:, 0:hi - lo]),
                         reads=[f"P{pi}"], writes=["g_f"])
                units.append(fgrp)
            QUNITS = []
            for cg in range(2):
                for tgi in range(2):
                    def qgrp(cg=cg, tgi=tgi):
                        lo, hi = 128 + tgi * 512, 128 + (tgi + 1) * 512
                        qpi = (4 + tgi) if OLDQ else 5
                        for hc in range(HC):
                            P.op("tensor", lambda t, cg=cg, hc=hc, lo=lo, hi=hi, par=par, qpi=qpi:
                                 t.matmul(PB[qpi][:, 0:512], wset[("q", par)][:, hc, cg * 128:(cg + 1) * 128],
                                          x_bf[:, hc, lo:hi], start=(hc == 0), stop=(hc == HC - 1)),
                                 reads=[f"wq{par}", "x_bf"], writes=[f"P{qpi}"])
                        P.op("scalar", lambda sc, cg=cg, tgi=tgi, qpi=qpi:
                             sc.activation(q_b[:, cg, tgi * 512:(tgi + 1) * 512], PB[qpi][:, 0:512], AF.Silu),
                             reads=[f"P{qpi}"], writes=[f"qb{tgi}"])
                    QUNITS.append(qgrp)
            for tb in range(NTB):
                def igrp(tb=tb):
                    for half in range(2):
                        for hc in range(HC):
                            P.op("tensor", lambda t, tb=tb, half=half, hc=hc, par=par:
                                 t.matmul(PB[2][0:64, half * 256:(half + 1) * 256],
                                          x_bf[:, hc, tb * 128 + half * 64: tb * 128 + (half + 1) * 64],
                                          wset[("i", par)][:, hc, :],
                                          start=(hc == 0), stop=(hc == HC - 1)),
                                 reads=[f"wi{par}", "x_bf"], writes=[f"P2{half}"])
                    for half in range(2):
                        P.op("vector", lambda v, tb=tb, half=half:
                             v.tensor_copy(vpool[0:64, 2 * tb + half, :],
                                           PB[2][0:64, half * 256:(half + 1) * 256]),
                             reads=["P20", "P21"], writes=[f"vp{2 * tb + half}"])
                units.append(igrp)
            units.extend(QUNITS)
            return units

        def phase_b(si):
            tiles = [(cg, tgi) for cg in range(2) for tgi in range(3)]
            for cg, tgi in tiles:
                lo, hi = fslice(tgi)
                P.op("scalar", lambda sc, cg=cg, lo=lo, hi=hi:
                     sc.activation(k_b[:, cg, lo:hi], g_f[:, cg, lo:hi], AF.Sigmoid, scale=-1.0),
                     reads=["g_f"], writes=["k_b"])
            for cg, tgi in tiles:
                lo, hi = fslice(tgi)
                P.op("scalar", lambda sc, cg=cg, lo=lo, hi=hi:
                     sc.activation(tA[:, 0:hi - lo], g_f[:, cg, lo:hi], AF.Abs),
                     reads=["g_f"], writes=["tA"])
                P.op("scalar", lambda sc, lo=lo, hi=hi:
                     sc.activation(tA[:, 0:hi - lo], tA[:, 0:hi - lo], AF.Exp, scale=-1.0),
                     reads=["tA"], writes=["tA"])
                P.op("scalar", lambda sc, lo=lo, hi=hi:
                     sc.activation(tA[:, 0:hi - lo], tA[:, 0:hi - lo], AF.Ln, bias=1.0),
                     reads=["tA"], writes=["tA"])
                P.op("vector", lambda v, cg=cg, lo=lo, hi=hi:
                     v.tensor_scalar_min(g_f[:, cg, lo:hi], g_f[:, cg, lo:hi], 0.0),
                     reads=["g_f"], writes=["g_f"])
                P.op("vector", lambda v, cg=cg, lo=lo, hi=hi:
                     v.tensor_sub(g_f[:, cg, lo:hi], g_f[:, cg, lo:hi], tA[:, 0:hi - lo]),
                     reads=["g_f", "tA"], writes=["g_f"])
            P.op("vector", lambda v: v.tensor_tensor_scan(G2d, smask_s[:, :], g2d, 0.0, ALU.mult, ALU.add),
                 reads=["g_f", "smask"], writes=["G_f"])
            P.op("scalar", lambda sc: sc.activation(lq2d, G2d, AF.Exp, bias=cb[:, 0:1]),
                 reads=["G_f", "cb"], writes=["lq_b"])
            P.op("scalar", lambda sc: sc.activation(lk2d, G2d, AF.Exp, scale=-1.0),
                 reads=["G_f"], writes=["lk_b"])
            P.op("vector", lambda v: v.tensor_mul(k2d, k2d, lk2d),
                 reads=["k_b", "lk_b"], writes=["k_b"])
            for tgi in range(2):
                P.op("vector", lambda v, tgi=tgi: v.tensor_mul(
                        q_b[:, :, tgi * 512:(tgi + 1) * 512], q_b[:, :, tgi * 512:(tgi + 1) * 512],
                        lq_b[:, :, HALO + tgi * 512:HALO + (tgi + 1) * 512]),
                     reads=[f"qb{tgi}", "lq_b"], writes=[f"qb{tgi}"])
            P.op("vector", lambda v: v.memset(Hst[:, :], 0.0), writes=["Hst"])
            P.op("vector", lambda v: v.memset(Hb[:, :], 0.0), writes=["Hb"])
            for ch in range(NCH):
                t0 = ch * C
                for hd in range(2):
                    P.op("vector", lambda v, hd=hd, ch=ch, t0=t0:
                         v.tensor_scalar(lame[:, hd * NCH + ch:hd * NCH + ch + 1],
                                         lq_b[:, hd, t0 + 63:t0 + 64], SQRT_DK, None, ALU.mult),
                         reads=["lq_b"], writes=["lame"])
                    P.op("vector", lambda v, hd=hd, ch=ch, t0=t0:
                         v.tensor_scalar(khall[:, hd, t0:t0 + 64], k_b[:, hd, t0:t0 + 64],
                                         lame[:, hd * NCH + ch:hd * NCH + ch + 1], None, ALU.mult),
                         reads=["k_b", "lame"], writes=["khall"])

        def chunk_loop(si, next_units):
            first_guard = {0: ["xstg0"], 2: ["xstg1"], 4: ["xtmp"]}.get(si, [])
            sched = {}
            tail_units = []
            for k, u in enumerate(next_units):
                if k < 6:
                    tk = k
                elif k < 15:
                    tb = k - 6
                    tk = max(6 + tb, 2 * tb + 3)
                else:
                    tk = NCH + 10
                if tk <= NCH:
                    sched.setdefault(tk, []).append(u)
                else:
                    tail_units.append(u)

            def rec_s(ch):
                if ch < 2:
                    return
                t0 = ch * C
                ot = t0 - HALO
                tp = ch % 2
                for hd in range(2):
                    so = (hd * 2 + tp) * 64
                    P.op("tensor", lambda t, hd=hd, t0=t0, ot=ot:
                         t.matmul(PB[3][0:64, hd * 64:(hd + 1) * 64], k_b[:, hd, t0:t0 + 64],
                                  q_b[:, hd, ot:ot + 64], start=True, stop=True),
                         reads=["k_b", f"qb{min(ot // 512, 1)}"], writes=[f"P3{hd}"])
                    P.op("vector", lambda v, hd=hd, so=so: v.tensor_mul(
                            sbfs[:, so:so + 64], PB[3][0:64, hd * 64:(hd + 1) * 64], triu_s[:, :]),
                         reads=[f"P3{hd}", "triu"], writes=[f"sbf{hd}{tp}"])

            def rec_o(ch):
                t0 = ch * C
                ot = t0 - HALO
                tp = ch % 2
                for hd in range(2):
                    if ch >= 2:
                        so = (hd * 2 + tp) * 64
                        P.op("tensor", lambda t, ch=ch, hd=hd, so=so:
                             t.matmul(PB[0][:, hd * 64:(hd + 1) * 64], vpool[0:64, ch, hd * 128:(hd + 1) * 128],
                                      sbfs[:, so:so + 64], start=True, stop=False),
                             reads=[f"vp{ch}", f"sbf{hd}{tp}"], writes=[f"P0{hd}"])
                        P.op("tensor", lambda t, hd=hd, ot=ot:
                             t.matmul(PB[0][:, hd * 64:(hd + 1) * 64], Hb[:, hd * 128:(hd + 1) * 128],
                                      q_b[:, hd, ot:ot + 64], start=False, stop=True),
                             reads=["Hb", f"qb{min(ot // 512, 1)}"], writes=[f"P0{hd}"])
                    P.op("tensor", lambda t, hd=hd, t0=t0: t.transpose(
                            PBF[0:64, hd * 128:(hd + 1) * 128], khall[:, hd, t0:t0 + 64], idbf_s[:, :]),
                         reads=["khall", "idbf"], writes=[f"PBF{hd}"])
                for hd in range(2):
                    gh = 2 * si + hd
                    if ch >= 2:
                        g2 = first_guard if (ch == 2 and hd == 0) else []
                        P.op("vector", lambda v, gh=gh, hd=hd, ot=ot:
                             v.tensor_copy(o_bf[:, gh, ot:ot + 64], PB[0][:, hd * 64:(hd + 1) * 64]),
                             reads=["P00", "P01"], writes=["o_bf"] + g2)
                    P.op("scalar", lambda sc, hd=hd: sc.copy(
                            khts[:, hd * 128:(hd + 1) * 128], PBF[0:64, hd * 128:(hd + 1) * 128]),
                         reads=[f"PBF{hd}"], writes=[f"khts{hd}"])
                    P.op("tensor", lambda t, ch=ch, hd=hd:
                         t.matmul(PB[6][:, hd * 128:(hd + 1) * 128], khts[:, hd * 128:(hd + 1) * 128],
                                  vpool[0:64, ch, hd * 128:(hd + 1) * 128],
                                  start=True, stop=True),
                         reads=[f"khts{hd}", f"vp{ch}"], writes=[f"P6{hd}"])
                    P.op("vector", lambda v, hd=hd, ch=ch:
                         v.scalar_tensor_tensor(Hst[:, hd * 128:(hd + 1) * 128], Hst[:, hd * 128:(hd + 1) * 128],
                                                lame[:, hd * NCH + ch:hd * NCH + ch + 1],
                                                PB[6][:, hd * 128:(hd + 1) * 128], ALU.mult, ALU.add),
                         reads=["Hst", "lame", f"P6{hd}"], writes=["Hst"])
                    P.op("scalar", lambda sc, hd=hd:
                         sc.copy(Hb[:, hd * 128:(hd + 1) * 128], Hst[:, hd * 128:(hd + 1) * 128]),
                         reads=["Hst"], writes=["Hb"])

            for tick in range(NCH + 1):
                if tick < NCH:
                    rec_s(tick)
                if tick >= 1:
                    rec_o(tick - 1)
                for u in sched.get(tick, []):
                    u()
                mat_pump(3)
            for u in tail_units:
                u()

        wload(0)
        wload(1)
        mat_pump(4)
        for u in proj_units(0):
            u()
            mat_pump(1)
        phase_b(0)
        for si in range(8):
            if si + 2 <= 7:
                wload(si + 2)
            nxt = proj_units(si + 1) if si + 1 <= 7 else []
            if os.environ.get('BK_NOIL'):
                pass_units, nxt = nxt, []
            else:
                pass_units = []
            chunk_loop(si, nxt)
            for u in pass_units:
                u()
            if si + 1 <= 7:
                phase_b(si + 1)
        while True:
            f_ = next(mat_it, None)
            if f_ is None:
                break
            f_()

        P.new_block()

        wo_s = cvb(0, HC * 2048).rearrange("p (h c) -> p h c", c=2048)
        xr = [cvf(65536, 512), cvf(67584, 512)]
        x1t = [cvf(69632, 512), cvf(71680, 512)]
        cst_b = [cvb(73728, 512), cvb(74752, 512)]
        sq_b = [cvb(75776, 512), cvb(76800, 512)]

        for c4 in range(4):
            src = wo_d.ap()[:, c4 * 512:(c4 + 1) * 512].rearrange("(h p) c -> p h c", p=128)
            P.op("gpsimd", lambda g, src=src, c4=c4: g.dma_start(wo_s[:, :, c4 * 512:(c4 + 1) * 512], src),
                 writes=[f"wo_s{c4}"], dma=True, dsem="wo")

        for hc in range(NH):
            for tg in range(2):
                lo = tg * 512
                P.op("scalar", lambda sc, hc=hc, lo=lo, tg=tg:
                     sc.activation(sq_b[tg][:, :], o_bf[:, hc, lo:lo + 512], AF.Square),
                     reads=["o_bf"], writes=[f"sqb{tg}"])
                P.op("tensor", lambda t, hc=hc, tg=tg:
                     t.matmul(PB[4][32 * tg:32 * tg + 1, 0:512], onesb_s[:, :], sq_b[tg][:, :],
                              start=(hc == 0), stop=(hc == NH - 1)),
                     reads=[f"sqb{tg}", "onesb"], writes=[f"P4r{tg}"])
        for tg in range(2):
            lo = tg * 512
            P.op("scalar", lambda sc, lo=lo, tg=tg:
                 sc.activation(rows[0:1, lo:lo + 512], PB[4][32 * tg:32 * tg + 1, 0:512], AF.Sqrt,
                               scale=1.0 / 2048.0, bias=cb[0:1, 1:2]),
                 reads=[f"P4r{tg}", "cb"], writes=["rows"])
            P.op("vector", lambda v, lo=lo: v.reciprocal(rows[0:1, 1024 + lo:1024 + lo + 512],
                                                         rows[0:1, lo:lo + 512]),
                 reads=["rows"], writes=["rows"])
            P.op("tensor", lambda t, lo=lo, tg=tg:
                 t.matmul(PB[2 + tg][:, 0:512], ones1_s[:, :], rows[0:1, 1024 + lo:1024 + lo + 512],
                          start=True, stop=True),
                 reads=["rows", "ones1"], writes=[f"P{2 + tg}"])
            P.op("vector", lambda v, lo=lo, tg=tg: v.tensor_copy(rstdr[:, lo:lo + 512], PB[2 + tg][:, 0:512]),
                 reads=[f"P{2 + tg}"], writes=["rstdr"])
        for hc in range(NH):
            P.op("vector", lambda v, hc=hc:
                 v.scalar_tensor_tensor(o_bf[:, hc, :], o_bf[:, hc, :], gwc_s[:, hc:hc + 1],
                                        rstdr[:, :], ALU.mult, ALU.mult),
                 reads=["o_bf", "gwc", "rstdr"], writes=["o_bf"])

        pend = []
        for oc in range(HC):
            for tg in range(2):
                lo = tg * 512
                for hc in range(HC):
                    P.op("tensor", lambda t, oc=oc, hc=hc, lo=lo, tg=tg:
                         t.matmul(PB[tg][:, 0:512], wo_s[:, hc, oc * 128:(oc + 1) * 128],
                                  o_bf[:, hc, lo:lo + 512], start=(hc == 0), stop=(hc == HC - 1)),
                         reads=[f"wo_s{oc // 4}", "o_bf"], writes=[f"P{tg}"])
            for f_ in pend:
                f_()
            pend = []
            for tg in range(2):
                lo = tg * 512
                P.op("sync", lambda s, oc=oc, lo=lo, tg=tg:
                     s.dma_start(xr[tg][:, :], xfm_d.ap()[:, oc, HALO + lo:HALO + lo + 512]),
                     reads=["xfmd"], writes=[f"xr{tg}"], dma=True, dsem=f"xr{tg}")
                P.op("vector", lambda v, tg=tg:
                     v.tensor_add(x1t[tg][:, :], PB[tg][:, 0:512], xr[tg][:, :]),
                     reads=[f"P{tg}", f"xr{tg}"], writes=[f"x1t{tg}"])
                P.op("sync", lambda s, oc=oc, lo=lo, tg=tg:
                     s.dma_start(x1p_d.ap()[:, oc, lo:lo + 512], x1t[tg][:, :]),
                     reads=[f"x1t{tg}"], writes=["x1pd"], dma=True, dsem="x1p")
                P.op("vector", lambda v, tg=tg: v.tensor_copy(cst_b[tg][:, :], x1t[tg][:, :]),
                     reads=[f"x1t{tg}"], writes=[f"cst{tg}"])
                P.op("scalar", lambda sc, tg=tg: sc.activation(sq_b[tg][:, :], cst_b[tg][:, :], AF.Square),
                     reads=[f"cst{tg}"], writes=[f"sqb{tg}"])
                def stats(oc=oc, tg=tg):
                    P.op("tensor", lambda t, oc=oc, tg=tg:
                         t.matmul(PB[2][32 * tg:32 * tg + 1, 0:512], onesb_s[:, :], cst_b[tg][:, :],
                                  start=(oc == 0), stop=(oc == HC - 1)),
                         reads=[f"cst{tg}", "onesb"], writes=[f"P2s{tg}"])
                    P.op("tensor", lambda t, oc=oc, tg=tg:
                         t.matmul(PB[3][32 * tg:32 * tg + 1, 0:512], onesb_s[:, :], sq_b[tg][:, :],
                                  start=(oc == 0), stop=(oc == HC - 1)),
                         reads=[f"sqb{tg}", "onesb"], writes=[f"P3s{tg}"])
                pend.append(stats)
        for f_ in pend:
            f_()
        for tg in range(2):
            P.op("vector", lambda v, tg=tg: v.tensor_scalar(rows[0:1, tg * 512:tg * 512 + 512],
                                                            PB[2][32 * tg:32 * tg + 1, 0:512],
                                                            1.0 / 2048.0, None, ALU.mult),
                 reads=[f"P2s{tg}"], writes=["rows"])
            P.op("vector", lambda v, tg=tg: v.tensor_scalar(rows[0:1, 1024 + tg * 512:1536 + tg * 512],
                                                            PB[3][32 * tg:32 * tg + 1, 0:512],
                                                            1.0 / 2048.0, None, ALU.mult),
                 reads=[f"P3s{tg}"], writes=["rows"])
            P.op("vector", lambda v, tg=tg: v.tensor_mul(rstdr[0:1, 0:512],
                                                         rows[0:1, tg * 512:tg * 512 + 512],
                                                         rows[0:1, tg * 512:tg * 512 + 512]),
                 reads=["rows"], writes=["rstdr"])
            P.op("vector", lambda v, tg=tg: v.tensor_sub(rows[0:1, 1024 + tg * 512:1536 + tg * 512],
                                                         rows[0:1, 1024 + tg * 512:1536 + tg * 512],
                                                         rstdr[0:1, 0:512]),
                 reads=["rows", "rstdr"], writes=["rows"])
            P.op("scalar", lambda sc, tg=tg: sc.activation(rows[0:1, 1024 + tg * 512:1536 + tg * 512],
                                                           rows[0:1, 1024 + tg * 512:1536 + tg * 512],
                                                           AF.Sqrt, bias=cb[0:1, 1:2]),
                 reads=["rows", "cb"], writes=["rows"])
            P.op("vector", lambda v, tg=tg: v.reciprocal(rstdr[0:1, 512:1024],
                                                         rows[0:1, 1024 + tg * 512:1536 + tg * 512]),
                 reads=["rows"], writes=["rstdr"])
            P.op("vector", lambda v, tg=tg: v.tensor_copy(rows[0:1, 1024 + tg * 512:1536 + tg * 512],
                                                          rstdr[0:1, 512:1024]),
                 reads=["rstdr"], writes=["rows"])
        for tg in range(2):
            P.op("tensor", lambda t, tg=tg: t.matmul(PB[2 + tg][:, 0:512], ones1_s[:, :],
                                                     rows[0:1, tg * 512:tg * 512 + 512],
                                                     start=True, stop=True),
                 reads=["rows", "ones1"], writes=[f"P{2 + tg}", f"P2s{tg}", f"P3s{tg}"])
            P.op("tensor", lambda t, tg=tg: t.matmul(PB[tg][:, 0:512], ones1_s[:, :],
                                                     rows[0:1, 1024 + tg * 512:1536 + tg * 512],
                                                     start=True, stop=True),
                 reads=["rows", "ones1"], writes=[f"P{tg}"])
        for oc in range(HC):
            for tg in range(2):
                lo = tg * 512
                P.op("sync", lambda s, oc=oc, lo=lo, tg=tg:
                     s.dma_start(xr[tg][:, :], x1p_d.ap()[:, oc, lo:lo + 512]),
                     reads=["x1pd"], writes=[f"xr{tg}"], dma=True, dsem=f"x1r{tg}")
                P.op("vector", lambda v, tg=tg: v.tensor_sub(x1t[tg][:, :], xr[tg][:, :], PB[2 + tg][:, 0:512]),
                     reads=[f"xr{tg}", f"P{2 + tg}"], writes=[f"x1t{tg}"])
                P.op("vector", lambda v, tg=tg: v.tensor_mul(x1t[tg][:, :], x1t[tg][:, :], PB[tg][:, 0:512]),
                     reads=[f"x1t{tg}", f"P{tg}"], writes=[f"x1t{tg}"])
                P.op("sync", lambda s, oc=oc, lo=lo, tg=tg:
                     s.dma_start(x1n_d.ap()[:, oc, lo:lo + 512], x1t[tg][:, :]),
                     reads=[f"x1t{tg}"], writes=["x1nd"], dma=True, dsem="x1n")
                P.op("vector", lambda v, oc=oc, lo=lo, tg=tg:
                     v.tensor_copy(x1_bf[:, oc, lo:lo + 512], x1t[tg][:, :]),
                     reads=[f"x1t{tg}"], writes=["x1_bf"])

        P.new_block()

        slabh = [cvb(94208, 8 * 768).rearrange("p (h c) -> p h c", c=384),
                 cvb(106496, 8 * 768).rearrange("p (h c) -> p h c", c=384)]
        gsil = cvb(159744, GOB_PER * TOWN).rearrange("p (j t) -> p j t", t=TOWN)

        def slab_load(cc_o_t, r, half, tag):
            src = cc_o_t.ap()[r * H:(r + 1) * H, half * 384:(half + 1) * 384] \
                .rearrange("(c p) o -> p c o", p=128)
            P.op("sync", lambda s, src=src, half=half: s.dma_start(slabh[half][:, :, :], src),
                 reads=[tag], writes=[f"slab{half}"], dma=True, dsem=f"sl{half}")

        for r in range(8):
            nj = min(GOB_PER, NOB - r * GOB_PER)
            for half in range(2):
                slab_load(ccg_o, r, half, "ccg_o")
                for j in range(half * 3, min(nj, half * 3 + 3)):
                    for tg in range(2):
                        pi = 4 + tg
                        for hc in range(HC):
                            P.op("tensor", lambda t, j=j, hc=hc, tg=tg, pi=pi, half=half:
                                 t.matmul(PB[pi][:, 0:512], slabh[half][:, hc, (j % 3) * 128:(j % 3 + 1) * 128],
                                          x1_bf[:, hc, tg * 512:(tg + 1) * 512],
                                          start=(hc == 0), stop=(hc == HC - 1)),
                                 reads=[f"slab{half}", "x1_bf"], writes=[f"P{pi}"])
                        P.op("scalar", lambda sc, j=j, tg=tg, pi=pi:
                             sc.activation(gsil[:, j, tg * 512:(tg + 1) * 512], PB[pi][:, 0:512], AF.Silu),
                             reads=[f"P{pi}"], writes=["gsil"])
            for half in range(2):
                slab_load(ccu_o, r, half, "ccu_o")
                for j in range(half * 3, min(nj, half * 3 + 3)):
                    ob = r * GOB_PER + j
                    for tg in range(2):
                        par = tg
                        for hc in range(HC):
                            P.op("tensor", lambda t, j=j, hc=hc, tg=tg, par=par, half=half:
                                 t.matmul(PB[par][:, 0:512], slabh[half][:, hc, (j % 3) * 128:(j % 3 + 1) * 128],
                                          x1_bf[:, hc, tg * 512:(tg + 1) * 512],
                                          start=(hc == 0), stop=(hc == HC - 1)),
                                 reads=[f"slab{half}", "x1_bf"], writes=[f"P{par}"])
                        P.op("vector", lambda v, ob=ob, j=j, tg=tg, par=par:
                             v.tensor_mul(h_bf[:, ob, tg * 512:(tg + 1) * 512],
                                          gsil[:, j, tg * 512:(tg + 1) * 512], PB[par][:, 0:512]),
                             reads=["gsil", f"P{par}"], writes=["h_bf"])

        P.new_block()

        wd_s = [cvb(126976, NOB * 256).rearrange("p (c o) -> p c o", o=256),
                cvb(94208, NOB * 256).rearrange("p (c o) -> p c o", o=256)]
        x1r5 = [cvf(149504, 512), cvf(151552, 512), cvf(153600, 512), cvf(155648, 512)]
        x2t5 = [cvf(157696, 512), cvf(159744, 512)]
        c5_b = [cvb(161792, 512), cvb(162816, 512)]
        s5_b = [cvb(163840, 512), cvb(164864, 512)]
        outt = cvf(165888, 2048)
        xnrm = [cvf(174080, 128), cvf(174592, 128)]
        xnt = cvf(116224, 2048)

        pend5 = []
        it5 = 0
        for r in range(8):
            wp = r % 2
            srcd = ccd_o.ap()[r * INTER:(r + 1) * INTER, :].rearrange("(c p) o -> p c o", p=128)
            P.op("sync", lambda s, srcd=srcd, wp=wp: s.dma_start(wd_s[wp][:, :, :], srcd),
                 reads=["ccd_o"], writes=[f"wd{wp}"], dma=True, dsem=f"wd{wp}")
            for j in range(DOB_PER):
                ob = r * DOB_PER + j
                for tg in range(2):
                    par = tg
                    lo = tg * 512
                    lp = it5 % 4
                    it5 += 1
                    P.op("sync", lambda s, ob=ob, lo=lo, lp=lp:
                         s.dma_start(x1r5[lp][:, :], x1n_d.ap()[:, ob, lo:lo + 512]),
                         reads=["x1nd"], writes=[f"x1r5{lp}"], dma=True, dsem=f"x2l{lp}")
                    for ic in range(NOB):
                        P.op("tensor", lambda t, j=j, ic=ic, tg=tg, par=par, wp=wp:
                             t.matmul(PB[par][:, 0:512], wd_s[wp][:, ic, j * 128:(j + 1) * 128],
                                      h_bf[:, ic, tg * 512:(tg + 1) * 512],
                                      start=(ic == 0), stop=(ic == NOB - 1)),
                             reads=[f"wd{wp}", "h_bf"], writes=[f"P{par}"])
                    for f_ in pend5:
                        f_()
                    pend5 = []
                    P.op("vector", lambda v, par=par, lp=lp:
                         v.tensor_add(x2t5[par][:, :], PB[par][:, 0:512], x1r5[lp][:, :]),
                         reads=[f"P{par}", f"x1r5{lp}"], writes=[f"x2t{par}"])
                    P.op("sync", lambda s, ob=ob, lo=lo, par=par:
                         s.dma_start(x2p_d.ap()[:, ob, lo:lo + 512], x2t5[par][:, :]),
                         reads=[f"x2t{par}"], writes=["x2pd"], dma=True, dsem="x2p")
                    P.op("vector", lambda v, par=par: v.tensor_copy(c5_b[par][:, :], x2t5[par][:, :]),
                         reads=[f"x2t{par}"], writes=[f"c5{par}"])
                    P.op("scalar", lambda sc, par=par: sc.activation(s5_b[par][:, :], c5_b[par][:, :], AF.Square),
                         reads=[f"c5{par}"], writes=[f"s5{par}"])
                    def stats5(ob=ob, tg=tg, par=par):
                        P.op("tensor", lambda t, ob=ob, tg=tg, par=par:
                             t.matmul(PB[2][32 * tg:32 * tg + 1, 0:512], onesb_s[:, :], c5_b[par][:, :],
                                      start=(ob == 0), stop=(ob == HC - 1)),
                             reads=[f"c5{par}", "onesb"], writes=[f"P2s{tg}"])
                        P.op("tensor", lambda t, ob=ob, tg=tg, par=par:
                             t.matmul(PB[3][32 * tg:32 * tg + 1, 0:512], onesb_s[:, :], s5_b[par][:, :],
                                      start=(ob == 0), stop=(ob == HC - 1)),
                             reads=[f"s5{par}", "onesb"], writes=[f"P3s{tg}"])
                    pend5.append(stats5)
        for f_ in pend5:
            f_()

        for tg in range(2):
            P.op("vector", lambda v, tg=tg: v.tensor_scalar(rows[0:1, tg * 512:tg * 512 + 512],
                                                            PB[2][32 * tg:32 * tg + 1, 0:512],
                                                            1.0 / 2048.0, None, ALU.mult),
                 reads=[f"P2s{tg}"], writes=["rows"])
            P.op("vector", lambda v, tg=tg: v.tensor_scalar(rows[0:1, 1024 + tg * 512:1536 + tg * 512],
                                                            PB[3][32 * tg:32 * tg + 1, 0:512],
                                                            1.0 / 2048.0, None, ALU.mult),
                 reads=[f"P3s{tg}"], writes=["rows"])
            P.op("vector", lambda v, tg=tg: v.tensor_mul(rstdr[0:1, 0:512],
                                                         rows[0:1, tg * 512:tg * 512 + 512],
                                                         rows[0:1, tg * 512:tg * 512 + 512]),
                 reads=["rows"], writes=["rstdr"])
            P.op("vector", lambda v, tg=tg: v.tensor_sub(rows[0:1, 1024 + tg * 512:1536 + tg * 512],
                                                         rows[0:1, 1024 + tg * 512:1536 + tg * 512],
                                                         rstdr[0:1, 0:512]),
                 reads=["rows", "rstdr"], writes=["rows"])
            P.op("scalar", lambda sc, tg=tg: sc.activation(rows[0:1, 1024 + tg * 512:1536 + tg * 512],
                                                           rows[0:1, 1024 + tg * 512:1536 + tg * 512],
                                                           AF.Sqrt, bias=cb[0:1, 1:2]),
                 reads=["rows", "cb"], writes=["rows"])
            P.op("vector", lambda v, tg=tg: v.reciprocal(rstdr[0:1, 512 - tg * 512:1024 - tg * 512],
                                                         rows[0:1, 1024 + tg * 512:1536 + tg * 512]),
                 reads=["rows"], writes=["rstdr"])

        for tg in range(2):
            P.op("tensor", lambda t, tg=tg: t.matmul(PB[2][:, 0:512], ones1_s[:, :],
                                                     rows[0:1, tg * 512:tg * 512 + 512],
                                                     start=True, stop=True),
                 reads=["rows", "ones1"], writes=["P2", "P2s0", "P2s1"])
            P.op("tensor", lambda t, tg=tg: t.matmul(PB[3][:, 0:512], ones1_s[:, :],
                                                     rstdr[0:1, 512 - tg * 512:1024 - tg * 512],
                                                     start=True, stop=True),
                 reads=["rstdr", "ones1"], writes=["P3", "P3s0", "P3s1"])
            for tbq in range(4):
                tb = tg * 4 + tbq
                P.op("sync", lambda s, tb=tb: s.dma_start(
                        xnt.rearrange("p (h t) -> p h t", t=128), x2p_d.ap()[:, :, tb * 128:(tb + 1) * 128]),
                     reads=["x2pd"], writes=["xnt"], dma=True, dsem="x2r")
                for oc in range(HC):
                    par = oc % 2
                    P.op("vector", lambda v, oc=oc, par=par, tbq=tbq:
                         v.tensor_sub(xnrm[par][:, :], xnt[:, oc * 128:(oc + 1) * 128],
                                      PB[2][:, tbq * 128:(tbq + 1) * 128]),
                         reads=["xnt", "P2"], writes=[f"xn{par}"])
                    P.op("vector", lambda v, par=par, tbq=tbq:
                         v.tensor_mul(xnrm[par][:, :], xnrm[par][:, :],
                                      PB[3][:, tbq * 128:(tbq + 1) * 128]),
                         reads=[f"xn{par}", "P3"], writes=[f"xn{par}"])
                    P.op("tensor", lambda t, par=par: t.transpose(PB[6][:, par * 128:(par + 1) * 128],
                                                                 xnrm[par][:, :], idf_s[:, :]),
                         reads=[f"xn{par}", "idf"], writes=[f"P6{par}"])
                    P.op("vector", lambda v, oc=oc, par=par: v.tensor_copy(outt[:, oc * 128:(oc + 1) * 128],
                                                                  PB[6][:, par * 128:(par + 1) * 128]),
                         reads=[f"P6{par}"], writes=["outt"])
                P.op("sync", lambda s, tb=tb: s.dma_start(y_d.ap()[tb * 128:(tb + 1) * 128, :], outt[:, :]),
                     reads=["outt"], dma=True, dsem="out")
        outsem = P._dsem("out")
        P.op("sync", lambda s: s.wait_ge(outsem, 16 * 8))

        P.emit()
    nc.finalize()
    return nc


_CACHE = {}


def kernel(L0_attn_x, L0_attn_Wq, L0_attn_Wf, L0_attn_Wi, L0_attn_Wo,
           L0_attn_g_weight, L0_ffn_lut, L0_ffn_gate_idx, L0_ffn_up_idx,
           L0_ffn_down_idx):
    x = np.ascontiguousarray(np.asarray(L0_attn_x, np.float32))
    wq = np.ascontiguousarray(np.asarray(L0_attn_Wq, np.float32))
    wf = np.ascontiguousarray(np.asarray(L0_attn_Wf, np.float32))
    wi = np.ascontiguousarray(np.asarray(L0_attn_Wi, np.float32))
    wo = np.ascontiguousarray(np.asarray(L0_attn_Wo, np.float32))
    gw = np.asarray(L0_attn_g_weight, np.float32)
    lut = np.asarray(L0_ffn_lut, np.float32).reshape(1, 16)
    gidx = np.asarray(L0_ffn_gate_idx, np.int32)
    uidx = np.asarray(L0_ffn_up_idx, np.int32)
    didx = np.asarray(L0_ffn_down_idx, np.int32)

    gwc = np.ascontiguousarray(gw.reshape(16, 128).T)
    gT = np.zeros((H, 48 * 128), np.int32)
    gT[:, :INTER] = gidx.T
    uT = np.zeros((H, 48 * 128), np.int32)
    uT[:, :INTER] = uidx.T
    dT = np.ascontiguousarray(didx.T)

    if "nc" not in _CACHE:
        _CACHE["nc"] = build()
    nc = _CACHE["nc"]

    zero_halo = np.zeros((HALO, H), np.float32)
    in_maps = []
    for c in range(8):
        b, hf = c // 2, c % 2
        m = {
            "x_own": np.ascontiguousarray(x[b, hf * TOWN:(hf + 1) * TOWN]),
            "x_halo": zero_halo if hf == 0 else np.ascontiguousarray(x[b, TOWN - HALO:TOWN]),
            "wq": wq, "wf": wf, "wi": wi, "wo": wo,
            "gwc": gwc, "lut16": lut,
            "gidxs": np.ascontiguousarray(gT[:, c * 768:(c + 1) * 768]),
            "uidxs": np.ascontiguousarray(uT[:, c * 768:(c + 1) * 768]),
            "didxs": np.ascontiguousarray(dT[:, c * 256:(c + 1) * 256]),
        }
        in_maps.append(m)

    res = run_bass_kernel_spmd(nc, in_maps, core_ids=list(range(8)))
    out = np.empty((4, 2048, H), np.float32)
    for c in range(8):
        b, hf = c // 2, c % 2
        out[b, hf * TOWN:(hf + 1) * TOWN] = res.results[c]["y"]
    return out


# version 12 (final)
# speedup vs baseline: 1.0400x; 1.0178x over previous
import os
import numpy as np
import ml_dtypes
from contextlib import ExitStack
import concourse.bass as bass
import concourse.mybir as mybir
from concourse.bass_utils import run_bass_kernel_spmd

F32 = mybir.dt.float32
BF16 = mybir.dt.bfloat16
I32 = mybir.dt.int32
AF = mybir.ActivationFunctionType
ALU = mybir.AluOpType

ENGINES = ("sync", "tensor", "vector", "scalar", "gpsimd")

H = 2048
HC = 16
NH = 16
DK = 128
TOWN = 1024
HALO = 128
TT = TOWN + HALO
NTB = TT // 128
C = 64
NCH = TT // C
INTER = 5504
NOB = 43
GOB_PER = 6
DOB_PER = 2
LNC_Q = float(-0.5 * np.log(128.0))
SQRT_DK = float(np.sqrt(128.0))
EPS = 1e-5


class Prog:

    def __init__(self, nc, st):
        self.nc = nc
        self.st = st
        self.chain = {e: st.enter_context(nc.semaphore(f"ch_{e}")) for e in ENGINES}
        self.dsems = {}
        self.blocks = [[]]
        self.res = {}
        self.ops = []
        self.dtot = {}
        self.fenced = {}

    def _dsem(self, key):
        if key not in self.dsems:
            self.dsems[key] = self.st.enter_context(self.nc.semaphore(f"dq_{key}"))
        return self.dsems[key]

    def _key(self, item):
        return ("r", item)

    def op(self, eng, fn, reads=(), writes=(), dma=False, dsem=None, asyn=False):
        idx = len(self.ops)
        deps = set()
        rkeys = [self._key(r) for r in reads]
        wkeys = [self._key(w) for w in writes]
        for k in rkeys:
            v = self.res.get(k)
            if v and v["w"] is not None:
                deps.add(v["w"])
        for k in wkeys:
            v = self.res.get(k)
            if v:
                if v["w"] is not None:
                    deps.add(v["w"])
                deps.update(v["r"])
        o = dict(eng=eng, fn=fn, dma=dma, deps=deps, idx=idx, waited=False,
                 cnt=None, blk=len(self.blocks) - 1, asyn=asyn or dma,
                 dsem=(dsem or f"_dq_{eng}") if dma else None)
        if dma:
            self.dtot[o["dsem"]] = self.dtot.get(o["dsem"], 0) + 16
        self.ops.append(o)
        self.blocks[-1].append(o)
        for k in rkeys:
            self.res.setdefault(k, {"w": None, "r": []})["r"].append(idx)
        for k in wkeys:
            e = self.res.setdefault(k, {"w": None, "r": []})
            e["w"] = idx
            e["r"] = []
        return idx

    def new_block(self):
        pend = {k: v for k, v in self.dtot.items() if v > self.fenced.get(k, 0)}
        if pend:
            def fence(s, pend=dict(pend)):
                inst = None
                for k, v in pend.items():
                    inst = s.wait_ge(self._dsem(k), v)
                return inst
            self.op("sync", fence)
            self.fenced.update(pend)
        if self.blocks[-1]:
            self.blocks.append([])

    def emit(self):
        for o in self.ops:
            if o["dma"]:
                o["waited"] = True
            real = set()
            for d in o["deps"]:
                p = self.ops[d]
                if p["blk"] < o["blk"] and not p["asyn"]:
                    continue
                if (p["eng"] == "tensor" and o["eng"] == "tensor"
                        and not p["dma"] and not o["dma"]):
                    continue
                real.add(d)
                p["waited"] = True
            o["rdeps"] = real
        cnt = {e: 0 for e in ENGINES}
        dcnt = {}
        for o in self.ops:
            if o["dma"]:
                dcnt[o["dsem"]] = dcnt.get(o["dsem"], 0) + 16
                o["cnt"] = dcnt[o["dsem"]]
            elif o["waited"]:
                cnt[o["eng"]] += 1
                o["cnt"] = cnt[o["eng"]]
        run = {}
        for o in self.ops:
            o["_run"] = dict(run)
            if o["dma"]:
                run[o["dsem"]] = run.get(o["dsem"], 0) + 16
        for blk in self.blocks:
            hi = {}
            for o in blk:
                o["waits"] = []
                for d in sorted(o["rdeps"]):
                    p = self.ops[d]
                    if p["dma"]:
                        semkey = ("d", p["dsem"])
                        val = o["_run"].get(p["dsem"], p["cnt"])
                        sem = self._dsem(p["dsem"])
                    else:
                        semkey = ("c", p["eng"])
                        val = p["cnt"]
                        sem = self.chain[p["eng"]]
                    key = (o["eng"], semkey)
                    if hi.get(key, 0) >= val:
                        continue
                    hi[key] = val
                    o["waits"].append((sem, val))
            byeng = {}
            for o in blk:
                byeng.setdefault(o["eng"], []).append(o)
            with self.nc.Block() as block:
                for eng, ops in byeng.items():
                    def body(engine, ops=ops):
                        for o in ops:
                            for sem, val in o["waits"]:
                                engine.wait_ge(sem, val)
                            inst = o["fn"](engine)
                            if o["dma"]:
                                inst.then_inc(self._dsem(o["dsem"]), 16)
                            elif o["waited"]:
                                inst.then_inc(self.chain[o["eng"]], 1)
                    getattr(block, eng)(body)


def build():
    nc = bass.Bass("TRN2", target_bir_lowering=False)

    x_own = nc.dram_tensor("x_own", [TOWN, H], F32, kind="ExternalInput")
    x_halo = nc.dram_tensor("x_halo", [HALO, H], F32, kind="ExternalInput")
    wq_d = nc.dram_tensor("wq", [H, H], F32, kind="ExternalInput")
    wf_d = nc.dram_tensor("wf", [H, H], F32, kind="ExternalInput")
    wi_d = nc.dram_tensor("wi", [H, H], F32, kind="ExternalInput")
    wo_d = nc.dram_tensor("wo", [H, H], F32, kind="ExternalInput")
    gwc_d = nc.dram_tensor("gwc", [128, 16], F32, kind="ExternalInput")
    lut_d = nc.dram_tensor("lut16", [1, 16], F32, kind="ExternalInput")
    gidx_d = nc.dram_tensor("gidxs", [H, GOB_PER * 128], I32, kind="ExternalInput")
    uidx_d = nc.dram_tensor("uidxs", [H, GOB_PER * 128], I32, kind="ExternalInput")
    didx_d = nc.dram_tensor("didxs", [INTER, DOB_PER * 128], I32, kind="ExternalInput")
    y_d = nc.dram_tensor("y", [TOWN, H], F32, kind="ExternalOutput")

    ccg_i = nc.dram_tensor("ccg_i", [H, GOB_PER * 128], BF16)
    ccu_i = nc.dram_tensor("ccu_i", [H, GOB_PER * 128], BF16)
    ccd_i = nc.dram_tensor("ccd_i", [INTER, DOB_PER * 128], BF16)
    ccg_o = nc.dram_tensor("ccg_o", [8 * H, GOB_PER * 128], BF16)
    ccu_o = nc.dram_tensor("ccu_o", [8 * H, GOB_PER * 128], BF16)
    ccd_o = nc.dram_tensor("ccd_o", [8 * INTER, DOB_PER * 128], BF16)
    xfm_d = nc.dram_tensor("xfm_d", [128, HC, TT], F32)
    x1p_d = nc.dram_tensor("x1p_d", [128, HC, TOWN], F32)
    x1n_d = nc.dram_tensor("x1n_d", [128, HC, TOWN], F32)
    x2p_d = nc.dram_tensor("x2p_d", [128, HC, TOWN], F32)

    ident_bf = nc.inline_tensor(np.eye(128, dtype=ml_dtypes.bfloat16), name="idbf")
    ident_f = nc.inline_tensor(np.eye(128, dtype=np.float32), name="idf")
    triu_c = nc.inline_tensor(np.triu(np.ones((64, 64), np.float32)), name="triu")
    onesb_c = nc.inline_tensor(np.ones((128, 1), ml_dtypes.bfloat16), name="onesb")
    ones1_c = nc.inline_tensor(np.ones((1, 128), np.float32), name="ones1")

    with ExitStack() as st:
        e = st.enter_context
        P = Prog(nc, st)

        AR_COLS = 44032
        arena = e(nc.sbuf_tensor("arena", [128, AR_COLS], F32))
        arf = arena.ap()
        arb = arena.ap().bitcast(BF16)

        def cvf(off_b, n):
            assert off_b % 4 == 0
            return arf[:, off_b // 4: off_b // 4 + n]

        def cvb(off_b, n):
            assert off_b % 2 == 0
            return arb[:, off_b // 2: off_b // 2 + n]

        idbf_s = e(nc.sbuf_tensor("idbf_s", [128, 128], BF16))
        idf_s = e(nc.sbuf_tensor("idf_s", [128, 128], F32))
        triu_s = e(nc.sbuf_tensor("triu_s", [64, 64], F32))
        onesb_s = e(nc.sbuf_tensor("onesb_s", [128, 1], BF16))
        ones1_s = e(nc.sbuf_tensor("ones1_s", [1, 128], F32))
        gwc_s = e(nc.sbuf_tensor("gwc_s", [128, 16], F32))
        lrow_s = e(nc.sbuf_tensor("lrow_s", [1, 16], F32))
        lrep_s = e(nc.sbuf_tensor("lrep_s", [128, 16], F32))
        smask_s = e(nc.sbuf_tensor("smask_s", [128, 2304], F32))
        Hst = e(nc.sbuf_tensor("Hst", [128, 256], F32))
        Hb = e(nc.sbuf_tensor("Hb", [128, 256], BF16))
        khs = e(nc.sbuf_tensor("khs", [128, 128], BF16))
        khts = e(nc.sbuf_tensor("khts", [64, 256], BF16))
        sbfs = e(nc.sbuf_tensor("sbfs", [64, 256], BF16))
        lame = e(nc.sbuf_tensor("lame", [128, 36], F32))
        rows = e(nc.sbuf_tensor("rows", [1, 2048], F32))
        rstdr = e(nc.sbuf_tensor("rstdr", [128, 1024], F32))
        cb = e(nc.sbuf_tensor("cb", [128, 2], F32))
        mat_i = e(nc.sbuf_tensor("mat_i", [128, 1536], BF16))
        mat_a = e(nc.sbuf_tensor("mat_a", [128, 1536], BF16))
        mat_t = e(nc.sbuf_tensor("mat_t", [128, 1536], BF16))

        PB = [e(nc.psum_tensor(f"pb{i}", [128, 512], F32)) for i in range(7)]
        PBF = e(nc.psum_tensor("pbf", [128, 1024], BF16))

        o_bf = cvb(94208, NH * TOWN).rearrange("p (h t) -> p h t", t=TOWN)
        x1_bf = cvb(126976, HC * TOWN).rearrange("p (h t) -> p h t", t=TOWN)
        h_bf = cvb(0, NOB * TOWN).rearrange("p (h t) -> p h t", t=TOWN)
        x_bf = cvb(0, HC * TT).rearrange("p (h t) -> p h t", t=TT)
        xstg = [cvf(94208, 2048), cvf(102400, 2048)]
        xtmp = cvf(110592, 2048)

        wset = {}
        for i, w in enumerate(("f", "q", "i")):
            for par in range(2):
                off = 45056 + (i * 2 + par) * 8192
                wset[(w, par)] = cvb(off, HC * 256).rearrange("p (h c) -> p h c", c=256)
        khall = cvb(36864, 2 * TT).rearrange("p (h t) -> p h t", t=TT)
        g_f = cvf(126976, 2 * TT).rearrange("p (h t) -> p h t", t=TT)
        g2d = cvf(126976, 2 * TT)
        G_f = cvf(136192, 2 * TT).rearrange("p (h t) -> p h t", t=TT)
        G2d = cvf(136192, 2 * TT)
        q_b = cvb(145408, 2 * TOWN).rearrange("p (h t) -> p h t", t=TOWN)
        vpool = cvb(149504, NCH * 256).rearrange("p (c v) -> p c v", v=256)
        lq_b = cvb(159744, 2 * TT).rearrange("p (h t) -> p h t", t=TT)
        lk_b = cvb(164352, 2 * TT).rearrange("p (h t) -> p h t", t=TT)
        k_b = cvb(168960, 2 * TT).rearrange("p (h t) -> p h t", t=TT)
        lq2d = cvb(159744, 2 * TT)
        lk2d = cvb(164352, 2 * TT)
        k2d = cvb(168960, 2 * TT)
        tA = cvf(173568, 512)

        P.op("sync", lambda s: s.dma_start(idbf_s[:, :], ident_bf.ap()), writes=["idbf"], dma=True, dsem="cst")
        P.op("sync", lambda s: s.dma_start(idf_s[:, :], ident_f.ap()), writes=["idf"], dma=True, dsem="cst")
        P.op("sync", lambda s: s.dma_start(triu_s[:, :], triu_c.ap()), writes=["triu"], dma=True, dsem="cst")
        P.op("sync", lambda s: s.dma_start(onesb_s[:, :], onesb_c.ap()), writes=["onesb"], dma=True, dsem="cst")
        P.op("sync", lambda s: s.dma_start(ones1_s[:, :], ones1_c.ap()), writes=["ones1"], dma=True, dsem="cst")
        P.op("sync", lambda s: s.dma_start(gwc_s[:, :], gwc_d.ap()), writes=["gwc"], dma=True, dsem="cst")
        P.op("sync", lambda s: s.dma_start(lrow_s[:, :], lut_d.ap()), writes=["lrow"], dma=True, dsem="cst")
        P.op("tensor", lambda t: t.matmul(PB[6][:, 0:16], ones1_s[:, :], lrow_s[:, :], start=True, stop=True),
             reads=["ones1", "lrow"], writes=["P60"])
        P.op("vector", lambda v: v.tensor_copy(lrep_s[:, :], PB[6][:, 0:16]),
             reads=["P60"], writes=["lrep"])
        P.op("vector", lambda v: v.memset(smask_s[:, :], 1.0), writes=["smask"])
        sm3 = smask_s.ap().rearrange("p (c t) -> p c t", t=64)
        P.op("vector", lambda v: v.memset(sm3[:, :, 0:1], 0.0), reads=["smask"], writes=["smask"])
        P.op("vector", lambda v: v.memset(cb[:, 0:1], LNC_Q), writes=["cb"])
        P.op("vector", lambda v: v.memset(cb[:, 1:2], EPS), reads=["cb"], writes=["cb"])

        for tb in range(NTB):
            par = tb % 2
            src = x_halo.ap() if tb == 0 else x_own.ap()[(tb - 1) * 128: tb * 128, :]
            P.op("sync", lambda s, src=src, par=par: s.dma_start(xstg[par][:, :], src),
                 writes=[f"xstg{par}"], dma=True, dsem=f"xs{par}")
            for hc in range(HC):
                tp = hc % 2 if not os.environ.get("BK_XSEQ") else 0
                P.op("tensor", lambda t, par=par, hc=hc, tp=tp: t.transpose(
                        PB[6][:, tp * 128:(tp + 1) * 128], xstg[par][:, hc * 128:(hc + 1) * 128], idf_s[:, :]),
                     reads=[f"xstg{par}", "idf"], writes=[f"P6{tp}"])
                P.op("vector", lambda v, hc=hc, tp=tp: v.tensor_copy(
                        xtmp[:, hc * 128:(hc + 1) * 128], PB[6][:, tp * 128:(tp + 1) * 128]),
                     reads=[f"P6{tp}"], writes=["xtmp"])
                P.op("scalar", lambda s, hc=hc, tb=tb: s.copy(x_bf[:, hc, tb * 128:(tb + 1) * 128],
                                                              xtmp[:, hc * 128:(hc + 1) * 128]),
                     reads=["xtmp"], writes=["x_bf"])
            P.op("sync", lambda s, tb=tb: s.dma_start(
                    xfm_d.ap()[:, :, tb * 128:(tb + 1) * 128],
                    xtmp.rearrange("p (h t) -> p h t", t=128)),
                 reads=["xtmp"], writes=["xfmd"], dma=True, dsem="xfd")

        def mk_mat_ops(src_d, dst_d, r0, nr, ncols, tagp):
            n = nr * ncols
            src = src_d.ap()[r0 * 128:(r0 + nr) * 128, :].rearrange("(c p) o -> p c o", p=128)
            yield lambda: P.op("gpsimd", lambda g: g.dma_start(
                    mat_i[:, 0:n].rearrange("p (c o) -> p c o", o=ncols), src),
                 writes=["mati"], dma=True, dsem="mi")
            yield lambda: P.op("vector", lambda v: v.tensor_scalar(
                    mat_a[:, 0:n], mat_i[:, 0:n], 0.0, lrep_s[:, 0:1], ALU.is_equal, ALU.mult),
                 reads=["mati", "lrep"], writes=["mata"])
            for j in range(1, 16):
                def two(j=j):
                    P.op("vector", lambda v: v.tensor_scalar(mat_t[:, 0:n], mat_i[:, 0:n], float(j),
                                                             lrep_s[:, j:j + 1], ALU.is_equal, ALU.mult),
                         reads=["mati", "lrep"], writes=["matt"])
                    P.op("vector", lambda v: v.tensor_add(mat_a[:, 0:n], mat_a[:, 0:n], mat_t[:, 0:n]),
                         reads=["mata", "matt"], writes=["mata"])
                yield two
            dst = dst_d.ap()[r0 * 128:(r0 + nr) * 128, :].rearrange("(c p) o -> p c o", p=128)
            yield lambda: P.op("sync", lambda s: s.dma_start(
                    dst, mat_a[:, 0:n].rearrange("p (c o) -> p c o", o=ncols)),
                 reads=["mata"], writes=[tagp + "_w"], dma=True, dsem=tagp)

        def mk_cc_ops(cin, cout, tagp, otag):
            yield lambda: P.op("gpsimd", lambda g: g.collective_compute(
                    "AllGather", ALU.bypass, replica_groups=[list(range(8))],
                    ins=[cin.ap().opt()], outs=[cout.ap().opt()]),
                 reads=[tagp + "_w"], writes=[otag], asyn=True)

        def mat_all():
            for r0 in range(0, HC, 2):
                yield from mk_mat_ops(gidx_d, ccg_i, r0, 2, 768, "ccg")
            yield from mk_cc_ops(ccg_i, ccg_o, "ccg", "ccg_o")
            for r0 in range(0, HC, 2):
                yield from mk_mat_ops(uidx_d, ccu_i, r0, 2, 768, "ccu")
            yield from mk_cc_ops(ccu_i, ccu_o, "ccu", "ccu_o")
            for r0 in range(0, 42, 6):
                yield from mk_mat_ops(didx_d, ccd_i, r0, 6, 256, "ccd")
            yield from mk_mat_ops(didx_d, ccd_i, 42, 1, 256, "ccd")
            yield from mk_cc_ops(ccd_i, ccd_o, "ccd", "ccd_o")

        mat_it = iter(mat_all())

        def mat_pump(k=1):
            for _ in range(k):
                f_ = next(mat_it, None)
                if f_ is None:
                    return
                f_()

        def fslice(tg):
            return (0, 512) if tg == 0 else ((512, 1024) if tg == 1 else (1024, 1152))

        def wload(si):
            par = si % 2
            for w, wd in (("f", wf_d), ("q", wq_d), ("i", wi_d)):
                src = wd.ap()[:, si * 256:(si + 1) * 256].rearrange("(h p) c -> p h c", p=128)
                P.op("gpsimd", lambda g, w=w, par=par, src=src: g.dma_start(wset[(w, par)][:, :, :], src),
                     writes=[f"w{w}{par}"], dma=True, dsem=f"w{w}{par}")

        def proj_units(si):
            par = si % 2
            units = []
            OLDPS = bool(os.environ.get("BK_OLDPS"))
            OLDF = OLDPS or bool(os.environ.get("BK_OLDF"))
            OLDQ = OLDPS or bool(os.environ.get("BK_OLDQ"))
            tiles = [(cg, tgi) for cg in range(2) for tgi in range(3)]
            for ti, (cg, tgi) in enumerate(tiles):
                def fgrp(cg=cg, tgi=tgi, ti=ti):
                    lo, hi = fslice(tgi)
                    pi = 4 + ti % 2
                    pp = PB[pi]
                    wtags = [f"P{pi}"]
                    for hc in range(HC):
                        P.op("tensor", lambda t, cg=cg, hc=hc, lo=lo, hi=hi, pp=pp, par=par:
                             t.matmul(pp[:, 0:hi - lo], wset[("f", par)][:, hc, cg * 128:(cg + 1) * 128],
                                      x_bf[:, hc, lo:hi], start=(hc == 0), stop=(hc == HC - 1)),
                             reads=[f"wf{par}", "x_bf"], writes=wtags)
                    P.op("scalar", lambda sc, cg=cg, lo=lo, hi=hi, pp=pp, pi=pi:
                         sc.copy(g_f[:, cg, lo:hi], pp[:, 0:hi - lo]),
                         reads=[f"P{pi}"], writes=["g_f"])
                units.append(fgrp)
            QUNITS = []
            for cg in range(2):
                for tgi in range(2):
                    def qgrp(cg=cg, tgi=tgi):
                        lo, hi = 128 + tgi * 512, 128 + (tgi + 1) * 512
                        qpi = (4 + tgi) if OLDQ else 5
                        for hc in range(HC):
                            P.op("tensor", lambda t, cg=cg, hc=hc, lo=lo, hi=hi, par=par, qpi=qpi:
                                 t.matmul(PB[qpi][:, 0:512], wset[("q", par)][:, hc, cg * 128:(cg + 1) * 128],
                                          x_bf[:, hc, lo:hi], start=(hc == 0), stop=(hc == HC - 1)),
                                 reads=[f"wq{par}", "x_bf"], writes=[f"P{qpi}"])
                        P.op("scalar", lambda sc, cg=cg, tgi=tgi, qpi=qpi:
                             sc.activation(q_b[:, cg, tgi * 512:(tgi + 1) * 512], PB[qpi][:, 0:512], AF.Silu),
                             reads=[f"P{qpi}"], writes=[f"qb{tgi}"])
                    QUNITS.append(qgrp)
            for tb in range(NTB):
                def igrp(tb=tb):
                    for half in range(2):
                        for hc in range(HC):
                            P.op("tensor", lambda t, tb=tb, half=half, hc=hc, par=par:
                                 t.matmul(PB[2][0:64, half * 256:(half + 1) * 256],
                                          x_bf[:, hc, tb * 128 + half * 64: tb * 128 + (half + 1) * 64],
                                          wset[("i", par)][:, hc, :],
                                          start=(hc == 0), stop=(hc == HC - 1)),
                                 reads=[f"wi{par}", "x_bf"], writes=[f"P2{half}"])
                    for half in range(2):
                        P.op("vector", lambda v, tb=tb, half=half:
                             v.tensor_copy(vpool[0:64, 2 * tb + half, :],
                                           PB[2][0:64, half * 256:(half + 1) * 256]),
                             reads=["P20", "P21"], writes=[f"vp{2 * tb + half}"])
                units.append(igrp)
            units.extend(QUNITS)
            return units

        def phase_b_safe_units(si):
            units = []
            tiles = [(cg, tgi) for cg in range(2) for tgi in range(3)]
            for cg, tgi in tiles:
                def gchain(cg=cg, tgi=tgi):
                    lo, hi = fslice(tgi)
                    P.op("scalar", lambda sc, cg=cg, lo=lo, hi=hi:
                         sc.activation(tA[:, 0:hi - lo], g_f[:, cg, lo:hi], AF.Abs),
                         reads=["g_f"], writes=["tA"])
                    P.op("scalar", lambda sc, lo=lo, hi=hi:
                         sc.activation(tA[:, 0:hi - lo], tA[:, 0:hi - lo], AF.Exp, scale=-1.0),
                         reads=["tA"], writes=["tA"])
                    P.op("scalar", lambda sc, lo=lo, hi=hi:
                         sc.activation(tA[:, 0:hi - lo], tA[:, 0:hi - lo], AF.Ln, bias=1.0),
                         reads=["tA"], writes=["tA"])
                    P.op("vector", lambda v, cg=cg, lo=lo, hi=hi:
                         v.tensor_scalar_min(g_f[:, cg, lo:hi], g_f[:, cg, lo:hi], 0.0),
                         reads=["g_f"], writes=["g_f"])
                    P.op("vector", lambda v, cg=cg, lo=lo, hi=hi:
                         v.tensor_sub(g_f[:, cg, lo:hi], g_f[:, cg, lo:hi], tA[:, 0:hi - lo]),
                         reads=["g_f", "tA"], writes=["g_f"])
                units.append(gchain)

            def scan_u():
                P.op("vector", lambda v: v.tensor_tensor_scan(G2d, smask_s[:, :], g2d, 0.0, ALU.mult, ALU.add),
                     reads=["g_f", "smask"], writes=["G_f"])
            units.append(scan_u)

            def lq_u():
                P.op("scalar", lambda sc: sc.activation(lq2d, G2d, AF.Exp, bias=cb[:, 0:1]),
                     reads=["G_f", "cb"], writes=["lq_b"])
            units.append(lq_u)

            def lk_u():
                P.op("scalar", lambda sc: sc.activation(lk2d, G2d, AF.Exp, scale=-1.0),
                     reads=["G_f"], writes=["lk_b"])
            units.append(lk_u)
            return units

        def phase_b_tail(si):
            P.op("scalar", lambda sc: sc.activation(k2d, g2d, AF.Exp),
                 reads=["g_f"], writes=["k_b"])
            P.op("vector", lambda v: v.tensor_scalar(k2d, k2d, -1.0, 1.0, ALU.mult, ALU.add),
                 reads=["k_b"], writes=["k_b"])
            P.op("vector", lambda v: v.tensor_mul(k2d, k2d, lk2d),
                 reads=["k_b", "lk_b"], writes=["k_b"])
            for tgi in range(2):
                P.op("vector", lambda v, tgi=tgi: v.tensor_mul(
                        q_b[:, :, tgi * 512:(tgi + 1) * 512], q_b[:, :, tgi * 512:(tgi + 1) * 512],
                        lq_b[:, :, HALO + tgi * 512:HALO + (tgi + 1) * 512]),
                     reads=[f"qb{tgi}", "lq_b"], writes=[f"qb{tgi}"])
            P.op("vector", lambda v: v.memset(Hst[:, :], 0.0), writes=["Hst"])
            P.op("vector", lambda v: v.memset(Hb[:, :], 0.0), writes=["Hb"])
            for ch in range(NCH):
                t0 = ch * C
                for hd in range(2):
                    P.op("vector", lambda v, hd=hd, ch=ch, t0=t0:
                         v.tensor_scalar(lame[:, hd * NCH + ch:hd * NCH + ch + 1],
                                         lq_b[:, hd, t0 + 63:t0 + 64], SQRT_DK, None, ALU.mult),
                         reads=["lq_b"], writes=["lame"])
                    P.op("vector", lambda v, hd=hd, ch=ch, t0=t0:
                         v.tensor_scalar(khall[:, hd, t0:t0 + 64], k_b[:, hd, t0:t0 + 64],
                                         lame[:, hd * NCH + ch:hd * NCH + ch + 1], None, ALU.mult),
                         reads=["k_b", "lame"], writes=["khall"])

        def chunk_loop(si, next_units, next_b_units):
            first_guard = {0: ["xstg0"], 2: ["xstg1"], 4: ["xtmp"]}.get(si, [])
            sched = {}
            tail_units = []
            for k, u in enumerate(next_units):
                if k < 6:
                    tk = k
                elif k < 15:
                    tb = k - 6
                    tk = max(6 + tb, 2 * tb + 3)
                else:
                    tk = NCH + 10
                if tk <= NCH:
                    sched.setdefault(tk, []).append(u)
                else:
                    tail_units.append(u)
            for k, u in enumerate(next_b_units):
                tk = 7 + k
                if tk <= NCH:
                    sched.setdefault(tk, []).append(u)
                else:
                    tail_units.append(u)

            def rec_s(ch):
                if ch < 2:
                    return
                t0 = ch * C
                ot = t0 - HALO
                tp = ch % 2
                for hd in range(2):
                    so = (hd * 2 + tp) * 64
                    P.op("tensor", lambda t, hd=hd, t0=t0, ot=ot:
                         t.matmul(PB[3][0:64, hd * 64:(hd + 1) * 64], k_b[:, hd, t0:t0 + 64],
                                  q_b[:, hd, ot:ot + 64], start=True, stop=True),
                         reads=["k_b", f"qb{min(ot // 512, 1)}"], writes=[f"P3{hd}"])
                    P.op("vector", lambda v, hd=hd, so=so: v.tensor_mul(
                            sbfs[:, so:so + 64], PB[3][0:64, hd * 64:(hd + 1) * 64], triu_s[:, :]),
                         reads=[f"P3{hd}", "triu"], writes=[f"sbf{hd}{tp}"])

            def rec_o(ch):
                t0 = ch * C
                ot = t0 - HALO
                tp = ch % 2
                for hd in range(2):
                    if ch >= 2:
                        so = (hd * 2 + tp) * 64
                        P.op("tensor", lambda t, ch=ch, hd=hd, so=so:
                             t.matmul(PB[0][:, hd * 64:(hd + 1) * 64], vpool[0:64, ch, hd * 128:(hd + 1) * 128],
                                      sbfs[:, so:so + 64], start=True, stop=False),
                             reads=[f"vp{ch}", f"sbf{hd}{tp}"], writes=[f"P0{hd}"])
                        P.op("tensor", lambda t, hd=hd, ot=ot:
                             t.matmul(PB[0][:, hd * 64:(hd + 1) * 64], Hb[:, hd * 128:(hd + 1) * 128],
                                      q_b[:, hd, ot:ot + 64], start=False, stop=True),
                             reads=["Hb", f"qb{min(ot // 512, 1)}"], writes=[f"P0{hd}"])
                    P.op("tensor", lambda t, hd=hd, t0=t0: t.transpose(
                            PBF[0:64, hd * 128:(hd + 1) * 128], khall[:, hd, t0:t0 + 64], idbf_s[:, :]),
                         reads=["khall", "idbf"], writes=[f"PBF{hd}"])
                for hd in range(2):
                    gh = 2 * si + hd
                    if ch >= 2:
                        g2 = first_guard if (ch == 2 and hd == 0) else []
                        P.op("vector", lambda v, gh=gh, hd=hd, ot=ot:
                             v.tensor_copy(o_bf[:, gh, ot:ot + 64], PB[0][:, hd * 64:(hd + 1) * 64]),
                             reads=["P00", "P01"], writes=["o_bf"] + g2)
                    P.op("scalar", lambda sc, hd=hd: sc.copy(
                            khts[:, hd * 128:(hd + 1) * 128], PBF[0:64, hd * 128:(hd + 1) * 128]),
                         reads=[f"PBF{hd}"], writes=[f"khts{hd}"])
                    P.op("tensor", lambda t, ch=ch, hd=hd:
                         t.matmul(PB[6][:, hd * 128:(hd + 1) * 128], khts[:, hd * 128:(hd + 1) * 128],
                                  vpool[0:64, ch, hd * 128:(hd + 1) * 128],
                                  start=True, stop=True),
                         reads=[f"khts{hd}", f"vp{ch}"], writes=[f"P6{hd}"])
                    P.op("vector", lambda v, hd=hd, ch=ch:
                         v.scalar_tensor_tensor(Hst[:, hd * 128:(hd + 1) * 128], Hst[:, hd * 128:(hd + 1) * 128],
                                                lame[:, hd * NCH + ch:hd * NCH + ch + 1],
                                                PB[6][:, hd * 128:(hd + 1) * 128], ALU.mult, ALU.add),
                         reads=["Hst", "lame", f"P6{hd}"], writes=["Hst"])
                    P.op("scalar", lambda sc, hd=hd:
                         sc.copy(Hb[:, hd * 128:(hd + 1) * 128], Hst[:, hd * 128:(hd + 1) * 128]),
                         reads=["Hst"], writes=["Hb"])

            for tick in range(NCH + 1):
                if tick < NCH:
                    rec_s(tick)
                if tick >= 1:
                    rec_o(tick - 1)
                for u in sched.get(tick, []):
                    u()
                mat_pump(3)
            for u in tail_units:
                u()

        wload(0)
        wload(1)
        mat_pump(4)
        for u in proj_units(0):
            u()
            mat_pump(1)
        for u in phase_b_safe_units(0):
            u()
        phase_b_tail(0)
        for si in range(8):
            if si + 2 <= 7:
                wload(si + 2)
            nxt = proj_units(si + 1) if si + 1 <= 7 else []
            nxtb = phase_b_safe_units(si + 1) if si + 1 <= 7 else []
            chunk_loop(si, nxt, nxtb)
            if si + 1 <= 7:
                phase_b_tail(si + 1)
        while True:
            f_ = next(mat_it, None)
            if f_ is None:
                break
            f_()

        P.new_block()

        wo_s = cvb(0, HC * 2048).rearrange("p (h c) -> p h c", c=2048)
        xr = [cvf(65536, 512), cvf(67584, 512)]
        x1t = [cvf(69632, 512), cvf(71680, 512)]
        cst_b = [cvb(73728, 512), cvb(74752, 512)]
        sq_b = [cvb(75776, 512), cvb(76800, 512)]

        for c4 in range(4):
            src = wo_d.ap()[:, c4 * 512:(c4 + 1) * 512].rearrange("(h p) c -> p h c", p=128)
            P.op("gpsimd", lambda g, src=src, c4=c4: g.dma_start(wo_s[:, :, c4 * 512:(c4 + 1) * 512], src),
                 writes=[f"wo_s{c4}"], dma=True, dsem="wo")

        for hc in range(NH):
            for tg in range(2):
                lo = tg * 512
                P.op("scalar", lambda sc, hc=hc, lo=lo, tg=tg:
                     sc.activation(sq_b[tg][:, :], o_bf[:, hc, lo:lo + 512], AF.Square),
                     reads=["o_bf"], writes=[f"sqb{tg}"])
                P.op("tensor", lambda t, hc=hc, tg=tg:
                     t.matmul(PB[4][32 * tg:32 * tg + 1, 0:512], onesb_s[:, :], sq_b[tg][:, :],
                              start=(hc == 0), stop=(hc == NH - 1)),
                     reads=[f"sqb{tg}", "onesb"], writes=[f"P4r{tg}"])
        for tg in range(2):
            lo = tg * 512
            P.op("scalar", lambda sc, lo=lo, tg=tg:
                 sc.activation(rows[0:1, lo:lo + 512], PB[4][32 * tg:32 * tg + 1, 0:512], AF.Sqrt,
                               scale=1.0 / 2048.0, bias=cb[0:1, 1:2]),
                 reads=[f"P4r{tg}", "cb"], writes=["rows"])
            P.op("vector", lambda v, lo=lo: v.reciprocal(rows[0:1, 1024 + lo:1024 + lo + 512],
                                                         rows[0:1, lo:lo + 512]),
                 reads=["rows"], writes=["rows"])
            P.op("tensor", lambda t, lo=lo, tg=tg:
                 t.matmul(PB[2 + tg][:, 0:512], ones1_s[:, :], rows[0:1, 1024 + lo:1024 + lo + 512],
                          start=True, stop=True),
                 reads=["rows", "ones1"], writes=[f"P{2 + tg}"])
            P.op("vector", lambda v, lo=lo, tg=tg: v.tensor_copy(rstdr[:, lo:lo + 512], PB[2 + tg][:, 0:512]),
                 reads=[f"P{2 + tg}"], writes=["rstdr"])
        for hc in range(NH):
            P.op("vector", lambda v, hc=hc:
                 v.scalar_tensor_tensor(o_bf[:, hc, :], o_bf[:, hc, :], gwc_s[:, hc:hc + 1],
                                        rstdr[:, :], ALU.mult, ALU.mult),
                 reads=["o_bf", "gwc", "rstdr"], writes=["o_bf"])

        pend = []
        for oc in range(HC):
            for tg in range(2):
                lo = tg * 512
                for hc in range(HC):
                    P.op("tensor", lambda t, oc=oc, hc=hc, lo=lo, tg=tg:
                         t.matmul(PB[tg][:, 0:512], wo_s[:, hc, oc * 128:(oc + 1) * 128],
                                  o_bf[:, hc, lo:lo + 512], start=(hc == 0), stop=(hc == HC - 1)),
                         reads=[f"wo_s{oc // 4}", "o_bf"], writes=[f"P{tg}"])
            for f_ in pend:
                f_()
            pend = []
            for tg in range(2):
                lo = tg * 512
                P.op("sync", lambda s, oc=oc, lo=lo, tg=tg:
                     s.dma_start(xr[tg][:, :], xfm_d.ap()[:, oc, HALO + lo:HALO + lo + 512]),
                     reads=["xfmd"], writes=[f"xr{tg}"], dma=True, dsem=f"xr{tg}")
                P.op("vector", lambda v, tg=tg:
                     v.tensor_add(x1t[tg][:, :], PB[tg][:, 0:512], xr[tg][:, :]),
                     reads=[f"P{tg}", f"xr{tg}"], writes=[f"x1t{tg}"])
                P.op("sync", lambda s, oc=oc, lo=lo, tg=tg:
                     s.dma_start(x1p_d.ap()[:, oc, lo:lo + 512], x1t[tg][:, :]),
                     reads=[f"x1t{tg}"], writes=["x1pd"], dma=True, dsem="x1p")
                P.op("vector", lambda v, tg=tg: v.tensor_copy(cst_b[tg][:, :], x1t[tg][:, :]),
                     reads=[f"x1t{tg}"], writes=[f"cst{tg}"])
                P.op("scalar", lambda sc, tg=tg: sc.activation(sq_b[tg][:, :], cst_b[tg][:, :], AF.Square),
                     reads=[f"cst{tg}"], writes=[f"sqb{tg}"])
                def stats(oc=oc, tg=tg):
                    P.op("tensor", lambda t, oc=oc, tg=tg:
                         t.matmul(PB[2][32 * tg:32 * tg + 1, 0:512], onesb_s[:, :], cst_b[tg][:, :],
                                  start=(oc == 0), stop=(oc == HC - 1)),
                         reads=[f"cst{tg}", "onesb"], writes=[f"P2s{tg}"])
                    P.op("tensor", lambda t, oc=oc, tg=tg:
                         t.matmul(PB[3][32 * tg:32 * tg + 1, 0:512], onesb_s[:, :], sq_b[tg][:, :],
                                  start=(oc == 0), stop=(oc == HC - 1)),
                         reads=[f"sqb{tg}", "onesb"], writes=[f"P3s{tg}"])
                pend.append(stats)
        for f_ in pend:
            f_()
        for tg in range(2):
            P.op("vector", lambda v, tg=tg: v.tensor_scalar(rows[0:1, tg * 512:tg * 512 + 512],
                                                            PB[2][32 * tg:32 * tg + 1, 0:512],
                                                            1.0 / 2048.0, None, ALU.mult),
                 reads=[f"P2s{tg}"], writes=["rows"])
            P.op("vector", lambda v, tg=tg: v.tensor_scalar(rows[0:1, 1024 + tg * 512:1536 + tg * 512],
                                                            PB[3][32 * tg:32 * tg + 1, 0:512],
                                                            1.0 / 2048.0, None, ALU.mult),
                 reads=[f"P3s{tg}"], writes=["rows"])
            P.op("vector", lambda v, tg=tg: v.tensor_mul(rstdr[0:1, 0:512],
                                                         rows[0:1, tg * 512:tg * 512 + 512],
                                                         rows[0:1, tg * 512:tg * 512 + 512]),
                 reads=["rows"], writes=["rstdr"])
            P.op("vector", lambda v, tg=tg: v.tensor_sub(rows[0:1, 1024 + tg * 512:1536 + tg * 512],
                                                         rows[0:1, 1024 + tg * 512:1536 + tg * 512],
                                                         rstdr[0:1, 0:512]),
                 reads=["rows", "rstdr"], writes=["rows"])
            P.op("scalar", lambda sc, tg=tg: sc.activation(rows[0:1, 1024 + tg * 512:1536 + tg * 512],
                                                           rows[0:1, 1024 + tg * 512:1536 + tg * 512],
                                                           AF.Sqrt, bias=cb[0:1, 1:2]),
                 reads=["rows", "cb"], writes=["rows"])
            P.op("vector", lambda v, tg=tg: v.reciprocal(rstdr[0:1, 512:1024],
                                                         rows[0:1, 1024 + tg * 512:1536 + tg * 512]),
                 reads=["rows"], writes=["rstdr"])
            P.op("vector", lambda v, tg=tg: v.tensor_copy(rows[0:1, 1024 + tg * 512:1536 + tg * 512],
                                                          rstdr[0:1, 512:1024]),
                 reads=["rstdr"], writes=["rows"])
        for tg in range(2):
            P.op("tensor", lambda t, tg=tg: t.matmul(PB[2 + tg][:, 0:512], ones1_s[:, :],
                                                     rows[0:1, tg * 512:tg * 512 + 512],
                                                     start=True, stop=True),
                 reads=["rows", "ones1"], writes=[f"P{2 + tg}", f"P2s{tg}", f"P3s{tg}"])
            P.op("tensor", lambda t, tg=tg: t.matmul(PB[tg][:, 0:512], ones1_s[:, :],
                                                     rows[0:1, 1024 + tg * 512:1536 + tg * 512],
                                                     start=True, stop=True),
                 reads=["rows", "ones1"], writes=[f"P{tg}"])
        ld4 = [xr[0], xr[1], x1t[0], x1t[1]]
        it3 = 0
        for oc in range(HC):
            for tg in range(2):
                lo = tg * 512
                lp = it3 % 4
                it3 += 1
                P.op("sync", lambda s, oc=oc, lo=lo, lp=lp:
                     s.dma_start(ld4[lp][:, :], x1p_d.ap()[:, oc, lo:lo + 512]),
                     reads=["x1pd"], writes=[f"ld4{lp}"], dma=True, dsem=f"x1r{lp}")
                P.op("vector", lambda v, tg=tg, lp=lp:
                     v.tensor_sub(ld4[lp][:, :], ld4[lp][:, :], PB[2 + tg][:, 0:512]),
                     reads=[f"ld4{lp}", f"P{2 + tg}"], writes=[f"ld4{lp}"])
                P.op("vector", lambda v, tg=tg, lp=lp:
                     v.tensor_mul(ld4[lp][:, :], ld4[lp][:, :], PB[tg][:, 0:512]),
                     reads=[f"ld4{lp}", f"P{tg}"], writes=[f"ld4{lp}"])
                P.op("sync", lambda s, oc=oc, lo=lo, lp=lp:
                     s.dma_start(x1n_d.ap()[:, oc, lo:lo + 512], ld4[lp][:, :]),
                     reads=[f"ld4{lp}"], writes=["x1nd"], dma=True, dsem="x1n")
                P.op("vector", lambda v, oc=oc, lo=lo, lp=lp:
                     v.tensor_copy(x1_bf[:, oc, lo:lo + 512], ld4[lp][:, :]),
                     reads=[f"ld4{lp}"], writes=["x1_bf"])

        P.new_block()

        slabh = [cvb(94208, 8 * 768).rearrange("p (h c) -> p h c", c=384),
                 cvb(106496, 8 * 768).rearrange("p (h c) -> p h c", c=384)]
        gsil = cvb(159744, GOB_PER * TOWN).rearrange("p (j t) -> p j t", t=TOWN)

        def slab_load(cc_o_t, r, half, tag):
            src = cc_o_t.ap()[r * H:(r + 1) * H, half * 384:(half + 1) * 384] \
                .rearrange("(c p) o -> p c o", p=128)
            P.op("sync", lambda s, src=src, half=half: s.dma_start(slabh[half][:, :, :], src),
                 reads=[tag], writes=[f"slab{half}"], dma=True, dsem=f"sl{half}")

        for r in range(8):
            nj = min(GOB_PER, NOB - r * GOB_PER)
            for half in range(2):
                slab_load(ccg_o, r, half, "ccg_o")
                for j in range(half * 3, min(nj, half * 3 + 3)):
                    for tg in range(2):
                        pi = 4 + tg
                        for hc in range(HC):
                            P.op("tensor", lambda t, j=j, hc=hc, tg=tg, pi=pi, half=half:
                                 t.matmul(PB[pi][:, 0:512], slabh[half][:, hc, (j % 3) * 128:(j % 3 + 1) * 128],
                                          x1_bf[:, hc, tg * 512:(tg + 1) * 512],
                                          start=(hc == 0), stop=(hc == HC - 1)),
                                 reads=[f"slab{half}", "x1_bf"], writes=[f"P{pi}"])
                        P.op("scalar", lambda sc, j=j, tg=tg, pi=pi:
                             sc.activation(gsil[:, j, tg * 512:(tg + 1) * 512], PB[pi][:, 0:512], AF.Silu),
                             reads=[f"P{pi}"], writes=["gsil"])
            for half in range(2):
                slab_load(ccu_o, r, half, "ccu_o")
                for j in range(half * 3, min(nj, half * 3 + 3)):
                    ob = r * GOB_PER + j
                    for tg in range(2):
                        par = tg
                        for hc in range(HC):
                            P.op("tensor", lambda t, j=j, hc=hc, tg=tg, par=par, half=half:
                                 t.matmul(PB[par][:, 0:512], slabh[half][:, hc, (j % 3) * 128:(j % 3 + 1) * 128],
                                          x1_bf[:, hc, tg * 512:(tg + 1) * 512],
                                          start=(hc == 0), stop=(hc == HC - 1)),
                                 reads=[f"slab{half}", "x1_bf"], writes=[f"P{par}"])
                        P.op("vector", lambda v, ob=ob, j=j, tg=tg, par=par:
                             v.tensor_mul(h_bf[:, ob, tg * 512:(tg + 1) * 512],
                                          gsil[:, j, tg * 512:(tg + 1) * 512], PB[par][:, 0:512]),
                             reads=["gsil", f"P{par}"], writes=["h_bf"])

        P.new_block()

        wd_s = [cvb(126976, NOB * 256).rearrange("p (c o) -> p c o", o=256),
                cvb(94208, NOB * 256).rearrange("p (c o) -> p c o", o=256)]
        x1r5 = [cvf(149504, 512), cvf(151552, 512), cvf(153600, 512), cvf(155648, 512)]
        x2t5 = [cvf(157696, 512), cvf(159744, 512)]
        c5_b = [cvb(161792, 512), cvb(162816, 512)]
        s5_b = [cvb(163840, 512), cvb(164864, 512)]
        outt = cvf(165888, 2048)
        xnt = cvf(116224, 2048)
        xnrm = [cvf(174080, 128), cvf(174592, 128)]

        pend5 = []
        it5 = 0
        for r in range(8):
            wp = r % 2
            srcd = ccd_o.ap()[r * INTER:(r + 1) * INTER, :].rearrange("(c p) o -> p c o", p=128)
            P.op("sync", lambda s, srcd=srcd, wp=wp: s.dma_start(wd_s[wp][:, :, :], srcd),
                 reads=["ccd_o"], writes=[f"wd{wp}"], dma=True, dsem=f"wd{wp}")
            for j in range(DOB_PER):
                ob = r * DOB_PER + j
                for tg in range(2):
                    par = tg
                    lo = tg * 512
                    lp = it5 % 4
                    it5 += 1
                    P.op("sync", lambda s, ob=ob, lo=lo, lp=lp:
                         s.dma_start(x1r5[lp][:, :], x1n_d.ap()[:, ob, lo:lo + 512]),
                         reads=["x1nd"], writes=[f"x1r5{lp}"], dma=True, dsem=f"x2l{lp}")
                    for ic in range(NOB):
                        P.op("tensor", lambda t, j=j, ic=ic, tg=tg, par=par, wp=wp:
                             t.matmul(PB[par][:, 0:512], wd_s[wp][:, ic, j * 128:(j + 1) * 128],
                                      h_bf[:, ic, tg * 512:(tg + 1) * 512],
                                      start=(ic == 0), stop=(ic == NOB - 1)),
                             reads=[f"wd{wp}", "h_bf"], writes=[f"P{par}"])
                    for f_ in pend5:
                        f_()
                    pend5 = []
                    P.op("vector", lambda v, par=par, lp=lp:
                         v.tensor_add(x2t5[par][:, :], PB[par][:, 0:512], x1r5[lp][:, :]),
                         reads=[f"P{par}", f"x1r5{lp}"], writes=[f"x2t{par}"])
                    P.op("sync", lambda s, ob=ob, lo=lo, par=par:
                         s.dma_start(x2p_d.ap()[:, ob, lo:lo + 512], x2t5[par][:, :]),
                         reads=[f"x2t{par}"], writes=["x2pd"], dma=True, dsem="x2p")
                    P.op("vector", lambda v, par=par: v.tensor_copy(c5_b[par][:, :], x2t5[par][:, :]),
                         reads=[f"x2t{par}"], writes=[f"c5{par}"])
                    P.op("scalar", lambda sc, par=par: sc.activation(s5_b[par][:, :], c5_b[par][:, :], AF.Square),
                         reads=[f"c5{par}"], writes=[f"s5{par}"])
                    def stats5(ob=ob, tg=tg, par=par):
                        P.op("tensor", lambda t, ob=ob, tg=tg, par=par:
                             t.matmul(PB[2][32 * tg:32 * tg + 1, 0:512], onesb_s[:, :], c5_b[par][:, :],
                                      start=(ob == 0), stop=(ob == HC - 1)),
                             reads=[f"c5{par}", "onesb"], writes=[f"P2s{tg}"])
                        P.op("tensor", lambda t, ob=ob, tg=tg, par=par:
                             t.matmul(PB[3][32 * tg:32 * tg + 1, 0:512], onesb_s[:, :], s5_b[par][:, :],
                                      start=(ob == 0), stop=(ob == HC - 1)),
                             reads=[f"s5{par}", "onesb"], writes=[f"P3s{tg}"])
                    pend5.append(stats5)
        for f_ in pend5:
            f_()

        for tg in range(2):
            P.op("vector", lambda v, tg=tg: v.tensor_scalar(rows[0:1, tg * 512:tg * 512 + 512],
                                                            PB[2][32 * tg:32 * tg + 1, 0:512],
                                                            1.0 / 2048.0, None, ALU.mult),
                 reads=[f"P2s{tg}"], writes=["rows"])
            P.op("vector", lambda v, tg=tg: v.tensor_scalar(rows[0:1, 1024 + tg * 512:1536 + tg * 512],
                                                            PB[3][32 * tg:32 * tg + 1, 0:512],
                                                            1.0 / 2048.0, None, ALU.mult),
                 reads=[f"P3s{tg}"], writes=["rows"])
            P.op("vector", lambda v, tg=tg: v.tensor_mul(rstdr[0:1, 0:512],
                                                         rows[0:1, tg * 512:tg * 512 + 512],
                                                         rows[0:1, tg * 512:tg * 512 + 512]),
                 reads=["rows"], writes=["rstdr"])
            P.op("vector", lambda v, tg=tg: v.tensor_sub(rows[0:1, 1024 + tg * 512:1536 + tg * 512],
                                                         rows[0:1, 1024 + tg * 512:1536 + tg * 512],
                                                         rstdr[0:1, 0:512]),
                 reads=["rows", "rstdr"], writes=["rows"])
            P.op("scalar", lambda sc, tg=tg: sc.activation(rows[0:1, 1024 + tg * 512:1536 + tg * 512],
                                                           rows[0:1, 1024 + tg * 512:1536 + tg * 512],
                                                           AF.Sqrt, bias=cb[0:1, 1:2]),
                 reads=["rows", "cb"], writes=["rows"])
            P.op("vector", lambda v, tg=tg: v.reciprocal(rstdr[0:1, 512 - tg * 512:1024 - tg * 512],
                                                         rows[0:1, 1024 + tg * 512:1536 + tg * 512]),
                 reads=["rows"], writes=["rstdr"])

        for tg in range(2):
            P.op("tensor", lambda t, tg=tg: t.matmul(PB[2][:, 0:512], ones1_s[:, :],
                                                     rows[0:1, tg * 512:tg * 512 + 512],
                                                     start=True, stop=True),
                 reads=["rows", "ones1"], writes=["P2", "P2s0", "P2s1"])
            P.op("tensor", lambda t, tg=tg: t.matmul(PB[3][:, 0:512], ones1_s[:, :],
                                                     rstdr[0:1, 512 - tg * 512:1024 - tg * 512],
                                                     start=True, stop=True),
                 reads=["rstdr", "ones1"], writes=["P3", "P3s0", "P3s1"])
            for tbq in range(4):
                tb = tg * 4 + tbq
                P.op("sync", lambda s, tb=tb: s.dma_start(
                        xnt.rearrange("p (h t) -> p h t", t=128), x2p_d.ap()[:, :, tb * 128:(tb + 1) * 128]),
                     reads=["x2pd"], writes=["xnt"], dma=True, dsem="x2r")
                for oc in range(HC):
                    par = oc % 2
                    P.op("vector", lambda v, oc=oc, par=par, tbq=tbq:
                         v.tensor_sub(xnrm[par][:, :], xnt[:, oc * 128:(oc + 1) * 128],
                                      PB[2][:, tbq * 128:(tbq + 1) * 128]),
                         reads=["xnt", "P2"], writes=[f"xn{par}"])
                    P.op("vector", lambda v, par=par, tbq=tbq:
                         v.tensor_mul(xnrm[par][:, :], xnrm[par][:, :],
                                      PB[3][:, tbq * 128:(tbq + 1) * 128]),
                         reads=[f"xn{par}", "P3"], writes=[f"xn{par}"])
                    P.op("tensor", lambda t, par=par: t.transpose(PB[6][:, par * 128:(par + 1) * 128],
                                                                 xnrm[par][:, :], idf_s[:, :]),
                         reads=[f"xn{par}", "idf"], writes=[f"P6{par}"])
                    P.op("vector", lambda v, oc=oc, par=par: v.tensor_copy(outt[:, oc * 128:(oc + 1) * 128],
                                                                  PB[6][:, par * 128:(par + 1) * 128]),
                         reads=[f"P6{par}"], writes=["outt"])
                P.op("sync", lambda s, tb=tb: s.dma_start(y_d.ap()[tb * 128:(tb + 1) * 128, :], outt[:, :]),
                     reads=["outt"], dma=True, dsem="out")
        outsem = P._dsem("out")
        P.op("sync", lambda s: s.wait_ge(outsem, 16 * 8))

        P.emit()
    nc.finalize()
    return nc


_CACHE = {}


def kernel(L0_attn_x, L0_attn_Wq, L0_attn_Wf, L0_attn_Wi, L0_attn_Wo,
           L0_attn_g_weight, L0_ffn_lut, L0_ffn_gate_idx, L0_ffn_up_idx,
           L0_ffn_down_idx):
    x = np.ascontiguousarray(np.asarray(L0_attn_x, np.float32))
    wq = np.ascontiguousarray(np.asarray(L0_attn_Wq, np.float32))
    wf = np.ascontiguousarray(np.asarray(L0_attn_Wf, np.float32))
    wi = np.ascontiguousarray(np.asarray(L0_attn_Wi, np.float32))
    wo = np.ascontiguousarray(np.asarray(L0_attn_Wo, np.float32))
    gw = np.asarray(L0_attn_g_weight, np.float32)
    lut = np.asarray(L0_ffn_lut, np.float32).reshape(1, 16)
    gidx = np.asarray(L0_ffn_gate_idx, np.int32)
    uidx = np.asarray(L0_ffn_up_idx, np.int32)
    didx = np.asarray(L0_ffn_down_idx, np.int32)

    gwc = np.ascontiguousarray(gw.reshape(16, 128).T)
    gT = np.zeros((H, 48 * 128), np.int32)
    gT[:, :INTER] = gidx.T
    uT = np.zeros((H, 48 * 128), np.int32)
    uT[:, :INTER] = uidx.T
    dT = np.ascontiguousarray(didx.T)

    if "nc" not in _CACHE:
        _CACHE["nc"] = build()
    nc = _CACHE["nc"]

    zero_halo = np.zeros((HALO, H), np.float32)
    in_maps = []
    for c in range(8):
        b, hf = c // 2, c % 2
        m = {
            "x_own": np.ascontiguousarray(x[b, hf * TOWN:(hf + 1) * TOWN]),
            "x_halo": zero_halo if hf == 0 else np.ascontiguousarray(x[b, TOWN - HALO:TOWN]),
            "wq": wq, "wf": wf, "wi": wi, "wo": wo,
            "gwc": gwc, "lut16": lut,
            "gidxs": np.ascontiguousarray(gT[:, c * 768:(c + 1) * 768]),
            "uidxs": np.ascontiguousarray(uT[:, c * 768:(c + 1) * 768]),
            "didxs": np.ascontiguousarray(dT[:, c * 256:(c + 1) * 256]),
        }
        in_maps.append(m)

    res = run_bass_kernel_spmd(nc, in_maps, core_ids=list(range(8)))
    out = np.empty((4, 2048, H), np.float32)
    for c in range(8):
        b, hf = c // 2, c % 2
        out[b, hf * TOWN:(hf + 1) * TOWN] = res.results[c]["y"]
    return out
